# version 03 (final)
# baseline (speedup 1.0000x reference)
import contextlib
import math
import numpy as np
import ml_dtypes
import concourse.bass as bass
import concourse.mybir as mybir
from concourse.bass_utils import run_bass_kernel_spmd

F32 = mybir.dt.float32
BF16 = mybir.dt.bfloat16
I32 = mybir.dt.int32
U8 = mybir.dt.uint8
AF = mybir.ActivationFunctionType
ALU = mybir.AluOpType
AX = mybir.AxisListType

ENGINES = ("tensor", "vector", "scalar", "gpsimd", "sync")
DMA_QUEUES = {"qgp": "gpsimd", "qsy": "sync", "qact": "scalar"}
DMA_LANES = 4


def _ap_key(x):
    if isinstance(x, str):
        return x
    if isinstance(x, tuple):
        return _ap_key(x[0]) + "#" + x[1]
    t = getattr(x, "tensor", None)
    if t is not None:
        return t.name
    name = getattr(x, "name", None)
    if name is not None:
        return name
    raise ValueError(f"cannot derive buffer key from {x!r}")


class Op:
    __slots__ = ("idx", "engine", "fn", "reads", "writes", "queue", "marked",
                 "deps", "waits", "mval", "tag", "lane")

    def __init__(self, idx, engine, fn, reads, writes, queue, tag):
        self.idx = idx
        self.engine = engine
        self.fn = fn
        self.reads = reads
        self.writes = writes
        self.queue = queue
        self.marked = False
        self.deps = []
        self.waits = []
        self.mval = 0
        self.tag = tag


class FW:

    def __init__(self, nc, st, paranoid=False, serial_dma=False):
        self.nc = nc
        self.st = st
        self.paranoid = paranoid
        self.serial_dma = serial_dma
        self.ops = []
        self.buf = {}
        self.fam = {}
        self.sb_tensors = {}
        self.psum_tensors = {}
        self.psum_banks_used = 0
        self.dram = {}
        self._sem_cache = {}

    def sbuf(self, name, shape, dtype):
        t = self.st.enter_context(self.nc.sbuf_tensor(name, list(shape), dtype))
        self.sb_tensors[name] = t
        return t

    def psum(self, name, shape, dtype):
        t = self.st.enter_context(self.nc.psum_tensor(name, list(shape), dtype))
        self.psum_tensors[name] = t
        nbytes = shape[1] * mybir.dt.size(dtype)
        self.psum_banks_used += (nbytes + 2047) // 2048
        assert self.psum_banks_used <= 8, "PSUM overflow"
        return t

    def dram_in(self, name, shape, dtype=F32):
        ap = self.nc.dram_tensor(name, list(shape), dtype, kind="ExternalInput").ap()
        self.dram[name] = ap
        return ap

    def dram_out(self, name, shape, dtype=F32):
        ap = self.nc.dram_tensor(name, list(shape), dtype, kind="ExternalOutput").ap()
        self.dram[name] = ap
        return ap

    def dram_scratch(self, name, shape, dtype=F32):
        ap = self.nc.dram_tensor(name, list(shape), dtype, kind="Internal").ap()
        self.dram[name] = ap
        return ap

    def op(self, engine, fn, reads=(), writes=(), tag=""):
        o = Op(len(self.ops), engine, fn, [_ap_key(r) for r in reads],
               [_ap_key(w) for w in writes], None, tag)
        self.ops.append(o)
        return o

    def dma(self, queue, out, in_, reads=(), writes=(), tag="", **kw):
        eng = DMA_QUEUES[queue]
        fn = lambda e: e.dma_start(out=out, in_=in_, **kw)
        o = Op(len(self.ops), eng, fn,
               [_ap_key(r) for r in (list(reads) or [in_])],
               [_ap_key(w) for w in (list(writes) or [out])], queue, tag)
        self.ops.append(o)
        return o

    def finalize(self):
        for o in self.ops:
            if self.paranoid and o.idx > 0:
                o.deps = [o.idx - 1]
                for k in o.reads:
                    w, rs = self.buf.get(k, (None, []))
                    self.buf[k] = (w, rs + [o.idx])
                for k in o.writes:
                    self.buf[k] = (o.idx, [])
                continue
            deps = set()
            if self.serial_dma and o.queue is not None:
                for j in range(o.idx - 1, -1, -1):
                    if self.ops[j].queue == o.queue:
                        deps.add(j)
                        break

            def base_of(k):
                return k.split("#", 1)[0] if "#" in k else k

            def family(k):
                b = base_of(k)
                if k == b:
                    return [m for m in self.fam.get(b, (b,))]
                return [k, b]

            for k in o.reads:
                for kk in family(k):
                    w, _ = self.buf.get(kk, (None, []))
                    if w is not None:
                        deps.add(w)
            for k in o.writes:
                for kk in family(k):
                    w, rs = self.buf.get(kk, (None, []))
                    if w is not None:
                        deps.add(w)
                    deps.update(rs)
            deps.discard(o.idx)
            o.deps = sorted(deps)
            for k in o.reads:
                b = base_of(k)
                self.fam.setdefault(b, {b}).add(k)
                w, rs = self.buf.get(k, (None, []))
                self.buf[k] = (w, rs + [o.idx])
            for k in o.writes:
                b = base_of(k)
                self.fam.setdefault(b, {b}).add(k)
                if k == b:
                    for m in list(self.fam.get(b, ())):
                        self.buf[m] = (o.idx, [])
                self.buf[k] = (o.idx, [])

        for o in self.ops:
            for j in o.deps:
                p = self.ops[j]
                if p.queue is None:
                    if p.engine == "tensor" and o.engine == "tensor" and o.queue is None:
                        continue
                    p.marked = True

        mcount = {e: 0 for e in ENGINES}
        qcount = {q: 0 for q in DMA_QUEUES}
        lcount = {(q, l): 0 for q in DMA_QUEUES for l in range(DMA_LANES)}
        for o in self.ops:
            if o.queue is not None:
                lane = qcount[o.queue] % DMA_LANES
                qcount[o.queue] += 1
                lcount[(o.queue, lane)] += 1
                o.lane = lane
                o.mval = lcount[(o.queue, lane)] * 16
            elif o.marked:
                mcount[o.engine] += 1
                o.mval = mcount[o.engine]
            else:
                o.mval = None
        self.final_qcount = dict(lcount)

        guar = {e: {} for e in ENGINES}
        for o in self.ops:
            need = {}
            for j in o.deps:
                p = self.ops[j]
                if p.queue is not None:
                    sk = ("q", p.queue, p.lane)
                    v = p.mval
                else:
                    if p.engine == "tensor" and o.engine == "tensor" and o.queue is None:
                        continue
                    sk = ("e", p.engine)
                    v = p.mval
                    assert v is not None
                if need.get(sk, 0) < v:
                    need[sk] = v
            g = guar[o.engine]
            o.waits = [(sk, v) for sk, v in sorted(need.items()) if g.get(sk, 0) < v]
            for sk, v in o.waits:
                g[sk] = v

    def emit(self):
        self.finalize()
        nc = self.nc
        esem = {e: self.st.enter_context(nc.semaphore(f"sem_{e}")) for e in ENGINES}
        qsem = {(q, l): self.st.enter_context(nc.semaphore(f"sem_{q}{l}"))
                for q in DMA_QUEUES for l in range(DMA_LANES)}

        def semh(sk):
            return esem[sk[1]] if sk[0] == "e" else qsem[(sk[1], sk[2])]

        per_engine = {e: [o for o in self.ops if o.engine == e] for e in ENGINES}
        block = self.st.enter_context(nc.Block())

        def run_engine(e, ops):
            def body(eng):
                for o in ops:
                    for sk, v in o.waits:
                        eng.wait_ge(semh(sk), v)
                    inst = o.fn(eng)
                    if o.queue is not None:
                        inst.then_inc(qsem[(o.queue, o.lane)], 16)
                    elif o.marked:
                        inst.then_inc(esem[o.engine], 1)
                if e == "sync":
                    for ql, c in self.final_qcount.items():
                        if c:
                            eng.wait_ge(qsem[ql], c * 16)
            return body

        for e in ENGINES:
            ops = per_engine[e]
            if not ops and e != "sync":
                continue
            getattr(block, e)(run_engine(e, ops))



def emit_matmul_acc(fw, psum_ap, pairs, psum_key=None, extra_reads=(), tag="mm",
                    extra_reads_fn=None):
    n = len(pairs)
    wkey = psum_key if psum_key is not None else psum_ap
    for i, (lhsT, rhs) in enumerate(pairs):
        er = list(extra_reads) + (extra_reads_fn(i) if extra_reads_fn else [])
        def fn(e, lhsT=lhsT, rhs=rhs, i=i):
            return e.matmul(psum_ap, lhsT, rhs, start=(i == 0), stop=(i == n - 1))
        fw.op("tensor", fn, reads=[lhsT, rhs, *er], writes=[wkey], tag=tag)


def np_to_bf16(a):
    return np.asarray(a, dtype=ml_dtypes.bfloat16)


T = 2048
D = 1024
KD = D // 128


class Ctx:
    def __init__(self, fw):
        self.fw = fw
        self.ones_col_f = fw.sbuf("ones_col_f", [128, 1], F32)
        self.ones_col_b = fw.sbuf("ones_col_b", [128, 1], BF16)
        self.ones_row_f = fw.sbuf("ones_row_f", [1, 128], F32)
        self.ones2_f = fw.sbuf("ones2_f", [128, 64], F32)
        self.eye_b = fw.sbuf("eye_b", [128, 128], BF16)
        self.eye_f = fw.sbuf("eye_f", [128, 128], F32)
        g = fw
        g.op("gpsimd", lambda e: e.memset(self.ones_col_f[:], 1.0), writes=[self.ones_col_f])
        g.op("gpsimd", lambda e: e.memset(self.ones_col_b[:], 1.0), writes=[self.ones_col_b])
        g.op("gpsimd", lambda e: e.memset(self.ones_row_f[:], 1.0), writes=[self.ones_row_f])
        g.op("gpsimd", lambda e: e.memset(self.ones2_f[:], 1.0), writes=[self.ones2_f])
        self._eye_built = False
        self._psum_pool = []
        self._psum_rr = 0
        self._scratch = {}
        self._consts = {}

    def rows(self):
        if "rows" not in self._scratch:
            self._scratch["rows"] = [self.fw.sbuf(f"ctx_row{i}", [1, T], F32) for i in range(2)]
        return self._scratch["rows"]

    def xstream(self):
        if "xstream" not in self._scratch:
            self._scratch["xstream"] = [self.fw.sbuf(f"ctx_xs{i}", [128, 512], F32)
                                        for i in range(3)]
        return self._scratch["xstream"]

    def sqs(self):
        if "sqs" not in self._scratch:
            self._scratch["sqs"] = [self.fw.sbuf(f"ctx_sq{i}", [128, 512], BF16) for i in range(2)]
        return self._scratch["sqs"]

    def constv(self, val, p=128):
        key = (float(val), p)
        if key not in self._consts:
            t = self.fw.sbuf(f"c_{p}_{len(self._consts)}", [p, 1], F32)
            self.fw.op("gpsimd", lambda e, t=t, val=val: e.memset(t[:], float(val)),
                       writes=[t])
            self._consts[key] = t
        return self._consts[key]

    def build_eye(self):
        if self._eye_built:
            return
        fw = self.fw
        rowi = fw.sbuf("eye_rowi", [128, 128], I32)
        coli = fw.sbuf("eye_coli", [128, 128], I32)
        eyei = fw.sbuf("eye_i", [128, 128], I32)
        fw.op("gpsimd", lambda e: e.iota(rowi[:], pattern=[[0, 128]], base=0, channel_multiplier=1),
              writes=[rowi])
        fw.op("gpsimd", lambda e: e.iota(coli[:], pattern=[[1, 128]], base=0, channel_multiplier=0),
              writes=[coli])
        fw.op("vector", lambda e: e.tensor_tensor(eyei[:], rowi[:], coli[:], ALU.is_equal),
              reads=[rowi, coli], writes=[eyei])
        fw.op("vector", lambda e: e.tensor_copy(self.eye_b[:], eyei[:]),
              reads=[eyei], writes=[self.eye_b])
        fw.op("vector", lambda e: e.tensor_copy(self.eye_f[:], eyei[:]),
              reads=[eyei], writes=[self.eye_f])
        self._eye_built = True

    def psum_bank(self, n=1, shape=None, dtype=F32):
        raise NotImplementedError

    def psum(self, name, shape=(128, 512), dtype=F32):
        return self.fw.psum(name, list(shape), dtype)

    def sbuf(self, name, shape, dtype):
        return self.fw.sbuf(name, shape, dtype)


class PsumPool:

    def __init__(self, fw, names, shape=(128, 512), dtype=F32, tiles=None):
        self.tiles = tiles if tiles is not None else [fw.psum(n, list(shape), dtype)
                                                      for n in names]
        self.i = 0

    def take(self):
        t = self.tiles[self.i % len(self.tiles)]
        self.i += 1
        return t

    def sub(self, n):
        return PsumPool(None, None, tiles=self.tiles[:n])



def emit_norm(ctx, xt, nt, *, kind="ln", eps=1e-5, nfeat=D, name="ln",
              pool=None, gamma=None, beta=None, t=T, xoff=0, xdram=None):
    fw = ctx.fw
    kd = KD if xt is None else len(xt)
    nchunk = (t + 511) // 512
    srow, vrow = ctx.rows()
    rrow = srow
    sq_scratch = ctx.sqs()
    assert pool is not None

    if xdram is not None:
        xbufs = ctx.xstream()
        xcnt = [0]

        def getx(k, c):
            b = xbufs[xcnt[0] % len(xbufs)]
            xcnt[0] += 1
            fw.dma("qsy", b[:], xdram[128 * k:128 * (k + 1),
                                      xoff + 512 * c: xoff + 512 * (c + 1)],
                   tag=f"{name}_lx")
            return b

        def xap(k, c):
            return getx(k, c)[:]

        def xrd(k, c, ap):
            return ap
    else:
        def xap(k, c):
            return xt[k][:, slice(xoff + 512 * c, xoff + 512 * (c + 1))]

        def xrd(k, c, ap):
            return xt[k]

    def ns(c):
        return slice(512 * c, 512 * (c + 1))

    xdt_b = (xdram is None and xt[0].dtype == BF16)
    ones_x = ctx.ones_col_b if xdt_b else ctx.ones_col_f
    if kind == "ln":
        for c in range(nchunk):
            ps = pool.take()
            for k in range(kd):
                a = xap(k, c)
                fw.op("tensor", lambda e, a=a, k=k, ps=ps: e.matmul(
                    ps[0:1, :], ones_x[:], a,
                    start=(k == 0), stop=(k == kd - 1)),
                    reads=[ones_x, xrd(k, c, a)], writes=[ps],
                    tag=f"{name}_sum")
            fw.op("vector", lambda e, ps=ps, c=c: e.tensor_scalar(
                srow[0:1, ns(c)], ps[0:1, :], -1.0 / nfeat, None, ALU.mult),
                reads=[ps], writes=[(srow, f"c{c}")], tag=f"{name}_nm")
        for c in range(nchunk):
            psb = pool.take()
            fw.op("tensor", lambda e, c=c, psb=psb: e.matmul(
                psb[:], ctx.ones_row_f[:], srow[0:1, ns(c)], start=True, stop=True),
                reads=[ctx.ones_row_f, (srow, f"c{c}")], writes=[psb], tag=f"{name}_bc")
            psv = pool.take()
            for k in range(kd):
                a = xap(k, c)
                fw.op("vector", lambda e, k=k, c=c, psb=psb, a=a: e.tensor_tensor(
                    nt[k][:, ns(c)], a, psb[:], ALU.add),
                    reads=[xrd(k, c, a), psb], writes=[(nt[k], f"c{c}")], tag=f"{name}_d")
            for k in range(kd):
                sq = sq_scratch[k % 2]
                fw.op("scalar", lambda e, k=k, c=c, sq=sq: e.activation(
                    sq[:], nt[k][:, ns(c)], AF.Square),
                    reads=[(nt[k], f"c{c}")], writes=[sq], tag=f"{name}_sq")
                fw.op("tensor", lambda e, psv=psv, k=k, sq=sq: e.matmul(
                    psv[0:1, :], ctx.ones_col_b[:], sq[:],
                    start=(k == 0), stop=(k == kd - 1)),
                    reads=[ctx.ones_col_b, sq], writes=[psv], tag=f"{name}_sqs")
            fw.op("vector", lambda e, psv=psv, c=c: e.tensor_scalar(
                vrow[0:1, ns(c)], psv[0:1, :], 1.0 / nfeat, None, ALU.mult),
                reads=[psv], writes=[(vrow, f"c{c}")], tag=f"{name}_v")
        vreads = [(vrow, f"c{c}") for c in range(nchunk)]
    elif kind in ("rms", "l2"):
        for c in range(nchunk):
            psv = pool.take()
            for k in range(kd):
                sq = sq_scratch[k % 2]
                a = xap(k, c)
                fw.op("scalar", lambda e, k=k, c=c, sq=sq, a=a: e.activation(
                    sq[:], a, AF.Square),
                    reads=[xrd(k, c, a)], writes=[sq], tag=f"{name}_sq")
                fw.op("tensor", lambda e, psv=psv, k=k, sq=sq: e.matmul(
                    psv[0:1, :], ctx.ones_col_b[:], sq[:],
                    start=(k == 0), stop=(k == kd - 1)),
                    reads=[ctx.ones_col_b, sq], writes=[psv], tag=f"{name}_sqs")
            scale = (1.0 / nfeat) if kind == "rms" else 1.0
            fw.op("vector", lambda e, psv=psv, c=c, scale=scale: e.tensor_scalar(
                vrow[0:1, ns(c)], psv[0:1, :], scale, None, ALU.mult),
                reads=[psv], writes=[(vrow, f"c{c}")], tag=f"{name}_v")
        vreads = [(vrow, f"c{c}") for c in range(nchunk)]
    else:
        raise ValueError(kind)

    ts_ = slice(0, t)
    if kind == "l2":
        fw.op("vector", lambda e: e.tensor_scalar(
            vrow[0:1, ts_], vrow[0:1, ts_], 1e-24, None, ALU.max),
            reads=vreads, writes=[vrow] + [(vrow, f"c{c}") for c in range(nchunk)],
            tag=f"{name}_clamp")
        fw.op("scalar", lambda e: e.activation(rrow[0:1, ts_], vrow[0:1, ts_], AF.Ln),
              reads=[vrow] + vreads, writes=[rrow], tag=f"{name}_lnv")
        post = math.sqrt(nfeat)
    else:
        epsc = ctx.constv(eps, p=1)
        fw.op("scalar", lambda e: e.activation(rrow[0:1, ts_], vrow[0:1, ts_], AF.Ln, bias=epsc[0:1, :]),
              reads=vreads + [epsc], writes=[rrow], tag=f"{name}_lnv")
        post = 1.0
    fw.op("scalar", lambda e: e.activation(
        rrow[0:1, ts_], rrow[0:1, ts_], AF.Exp, scale=-0.5),
        reads=[rrow], writes=[rrow], tag=f"{name}_exp")
    if post != 1.0:
        fw.op("vector", lambda e, post=post: e.tensor_scalar(
            rrow[0:1, ts_], rrow[0:1, ts_], float(post), None, ALU.mult),
            reads=[rrow], writes=[rrow], tag=f"{name}_post")

    for c in range(nchunk):
        psb = pool.take()
        fw.op("tensor", lambda e, c=c, psb=psb: e.matmul(
            psb[:], ctx.ones_row_f[:], rrow[0:1, ns(c)], start=True, stop=True),
            reads=[ctx.ones_row_f, rrow], writes=[psb], tag=f"{name}_bcr")
        for k in range(kd):
            if kind == "ln":
                sslice, srd = nt[k][:, ns(c)], (nt[k], f"c{c}")
            else:
                sslice = xap(k, c)
                srd = xrd(k, c, sslice)
            fw.op("vector", lambda e, k=k, c=c, psb=psb, sslice=sslice: e.tensor_tensor(
                nt[k][:, ns(c)], sslice, psb[:], ALU.mult),
                reads=[srd, psb], writes=[(nt[k], f"c{c}")], tag=f"{name}_r")
            if gamma is not None:
                if beta is not None:
                    fw.op("vector", lambda e, k=k, c=c: e.tensor_scalar(
                        nt[k][:, ns(c)], nt[k][:, ns(c)], gamma[k], beta[k],
                        ALU.mult, ALU.add),
                        reads=[(nt[k], f"c{c}"), gamma[k], beta[k]],
                        writes=[(nt[k], f"c{c}")], tag=f"{name}_g")
                else:
                    fw.op("vector", lambda e, k=k, c=c: e.tensor_scalar(
                        nt[k][:, ns(c)], nt[k][:, ns(c)], gamma[k], None, ALU.mult),
                        reads=[(nt[k], f"c{c}"), gamma[k]],
                        writes=[(nt[k], f"c{c}")], tag=f"{name}_g")



def pack2(wT, jtile=128):
    din, dout = wT.shape
    assert din % 128 == 0 and dout % 128 == 0
    kd, J = din // 128, dout // 128
    a = wT.reshape(kd, 128, J, 128).transpose(2, 1, 0, 3).reshape(J * 128, kd * 128)
    return np.ascontiguousarray(a)


def emit_cast_dram(fw, dst, src, nrows, width, name, queue_ld="qgp", queue_st="qsy",
                   bufs=None):
    if bufs is None:
        bufs = [fw.sbuf(f"{name}_cb{i}", [128, width], BF16) for i in range(2)]
    assert nrows % 128 == 0
    for r in range(nrows // 128):
        b = bufs[r % 2]
        fw.dma(queue_ld, b[:, :width], src[128 * r:128 * (r + 1), :], tag=f"{name}_cl")
        fw.dma(queue_st, dst[128 * r:128 * (r + 1), :], b[:, :width], tag=f"{name}_cs")


def emit_linear(ctx, pool, *, wsrc, kd, j0, j1, chunks, rhs_tile, evac,
                wbufs, queue="qsy", tag="lin"):
    fw = ctx.fw
    nb = len(wbufs)
    for j in range(j0, j1):
        wb = wbufs[j % nb]
        fw.dma(queue, wb[:], wsrc[128 * j:128 * (j + 1), :], tag=f"{tag}_ldw")
        for (c, cs) in chunks:
            ps = pool.take()
            for k in range(kd):
                fw.op("tensor", lambda e, wb=wb, k=k, c=c, cs=cs, ps=ps: e.matmul(
                    ps[:], wb[:, 128 * k:128 * (k + 1)], rhs_tile(k, c, cs),
                    start=(k == 0), stop=(k == kd - 1)),
                    reads=[wb, rhs_tile(k, c, cs)], writes=[ps], tag=tag)
            evac(j, c, cs, ps)



def emit_l2ffn(ctx, pool, *, xt, w1s, w2s, h1s, h2s, w3s, scales_sb, name):
    fw = ctx.fw
    QT = 512
    ntq = [fw.sbuf(f"{name}_nt{k}", [128, QT], BF16) for k in range(8)]
    uA = [fw.sbuf(f"{name}_uA{j}", [128, QT], BF16) for j in range(32)]
    uB = [fw.sbuf(f"{name}_uB{j}", [128, QT], BF16) for j in range(32)]
    w12buf = [fw.sbuf(f"{name}_w12_{i}", [128, 1024], BF16) for i in range(4)]
    hbuf = [fw.sbuf(f"{name}_h_{i}", [128, 4096], BF16) for i in range(2)]
    silu_t = [fw.sbuf(f"{name}_silu{i}", [128, QT], F32) for i in range(2)]
    s1, s2, sh1, sh2, s3h = (scales_sb["s1"], scales_sb["s2"], scales_sb["sh1"],
                             scales_sb["sh2"], scales_sb["s3h"])

    for q in range(4):
        qs = slice(QT * q, QT * (q + 1))
        emit_norm(ctx, xt, ntq, kind="ln", eps=1e-5, name=f"{name}_ln{q}",
                  pool=pool, t=QT, xoff=QT * q)
        for j in range(32):
            wb1 = w12buf[(2 * j) % 4]
            wb2 = w12buf[(2 * j + 1) % 4]
            fw.dma("qsy", wb1[:], w1s[128 * j:128 * (j + 1), :], tag=f"{name}_lw1")
            fw.dma("qsy", wb2[:], w2s[128 * j:128 * (j + 1), :], tag=f"{name}_lw2")
            psA = pool.take()
            for k in range(8):
                fw.op("tensor", lambda e, wb1=wb1, k=k, psA=psA: e.matmul(
                    psA[:], wb1[:, 128 * k:128 * (k + 1)], ntq[k][:, 0:QT],
                    start=(k == 0), stop=(k == 7)),
                    reads=[wb1, (ntq[k], "c0")], writes=[psA], tag=f"{name}_s1a")
            st = silu_t[j % 2]
            fw.op("scalar", lambda e, psA=psA, j=j, st=st: e.activation(
                st[:], psA[:], AF.Silu, scale=s1[:, j:j + 1]),
                reads=[psA, s1], writes=[st], tag=f"{name}_silu")
            psB = pool.take()
            for k in range(8):
                fw.op("tensor", lambda e, wb2=wb2, k=k, psB=psB: e.matmul(
                    psB[:], wb2[:, 128 * k:128 * (k + 1)], ntq[k][:, 0:QT],
                    start=(k == 0), stop=(k == 7)),
                    reads=[wb2, (ntq[k], "c0")], writes=[psB], tag=f"{name}_s1b")
            fw.op("vector", lambda e, psB=psB, j=j, st=st: e.scalar_tensor_tensor(
                uA[j][:], psB[:], s2[:, j:j + 1], st[:], ALU.mult, ALU.mult),
                reads=[psB, s2, st], writes=[uA[j]], tag=f"{name}_u1")
        def ev2(j, c, cs, ps):
            fw.op("scalar", lambda e, j=j, ps=ps: e.activation(
                uB[j][:], ps[:], AF.Gelu, scale=sh1[:, j:j + 1]),
                reads=[ps, sh1], writes=[uB[j]], tag=f"{name}_gelu1")
        emit_linear(ctx, pool, wsrc=h1s, kd=32, j0=0, j1=32,
                    chunks=[(0, slice(0, QT))],
                    rhs_tile=lambda k, c, cs: uA[k][:, cs],
                    evac=ev2, wbufs=hbuf, tag=f"{name}_h1")
        def ev3(j, c, cs, ps):
            fw.op("scalar", lambda e, j=j, ps=ps: e.activation(
                uA[j][:], ps[:], AF.Gelu, scale=sh2[:, j:j + 1]),
                reads=[ps, sh2], writes=[uA[j]], tag=f"{name}_gelu2")
        emit_linear(ctx, pool, wsrc=h2s, kd=32, j0=0, j1=32,
                    chunks=[(0, slice(0, QT))],
                    rhs_tile=lambda k, c, cs: uB[k][:, cs],
                    evac=ev3, wbufs=hbuf, tag=f"{name}_h2")
        def ev4(j, c, cs, ps):
            fw.op("vector", lambda e, j=j, ps=ps, qs=qs: e.scalar_tensor_tensor(
                xt[j][:, qs], ps[:], s3h[:, j:j + 1], xt[j][:, qs], ALU.mult, ALU.add),
                reads=[ps, s3h, xt[j]], writes=[xt[j]], tag=f"{name}_res")
        emit_linear(ctx, pool, wsrc=w3s, kd=32, j0=0, j1=8,
                    chunks=[(0, slice(0, QT))],
                    rhs_tile=lambda k, c, cs: uA[k][:, cs],
                    evac=ev4, wbufs=hbuf, tag=f"{name}_w3")




def emit_attention(ctx, pool, *, xdram, xodram, wqp, wkp, wvt, wop=None, bo=None,
                   hsel, scale, causal, margin, name, nheads=16, hd=64):
    fw = ctx.fw
    HPT = 128 // hd
    nt = [fw.sbuf(f"{name}_nt{k}", [128, T], BF16) for k in range(KD)]
    emit_norm(ctx, None, nt, kind="ln", eps=1e-5, name=f"{name}_ln", pool=pool,
              xdram=xdram)

    QT = [fw.sbuf(f"{name}_q{j}", [128, T], BF16) for j in range(KD)]
    KT = [fw.sbuf(f"{name}_k{j}", [128, T], BF16) for j in range(KD)]
    vo = [fw.sbuf(f"{name}_vo{i}", [128, nheads * (hd + 1)], BF16) for i in range(T // 128)]
    wqb = [fw.sbuf(f"{name}_wqb{i}", [128, 1024], BF16) for i in range(2)]
    wvb = [fw.sbuf(f"{name}_wvb{k}", [128, 1024], BF16) for k in range(KD)]
    PT = [fw.sbuf(f"{name}_pt{i}", [128, 512], BF16) for i in range(2)]
    otc = [fw.sbuf(f"{name}_otc{j}", [128, 512], BF16) for j in range(KD)]
    qn2 = fw.sbuf(f"{name}_qn2", [nheads, T], BF16)
    negc = fw.sbuf(f"{name}_negc", [nheads, T], BF16)
    kmx = fw.sbuf(f"{name}_kmx", [nheads, 1], F32)
    lrow = fw.sbuf(f"{name}_lrow", [65, 512], F32)
    rrec = lrow
    rbsb = fw.sbuf(f"{name}_rbsb", [64, 512], F32)
    onesrow_b = fw.sbuf(f"{name}_or_b", [1, 128], BF16)
    fw.op("gpsimd", lambda e: e.memset(onesrow_b[:], 1.0), writes=[onesrow_b])
    xres = ctx.xstream()[0:2]
    xout = [fw.sbuf(f"{name}_xo{i}", [128, 512], F32) for i in range(2)]
    if wop is not None:
        wob = [fw.sbuf(f"{name}_wob{i}", [128, 1024], BF16) for i in range(2)]

    nc4 = T // 512
    ntt = T // 128

    for (wsrc, dst, tg) in ((wqp, QT, "q"), (wkp, KT, "k")):
        for j in range(KD):
            wb = wqb[j % 2]
            fw.dma("qgp", wb[:], wsrc[128 * j:128 * (j + 1), :], tag=f"{name}_lw{tg}")
            for c in range(nc4):
                cs = slice(512 * c, 512 * (c + 1))
                ps = pool.take()
                for k in range(KD):
                    fw.op("tensor", lambda e, wb=wb, k=k, cs=cs, ps=ps: e.matmul(
                        ps[:], wb[:, 128 * k:128 * (k + 1)], nt[k][:, cs],
                        start=(k == 0), stop=(k == KD - 1)),
                        reads=[wb, (nt[k], f"c{c}")], writes=[ps], tag=f"{name}_p{tg}")
                fw.op("scalar", lambda e, j=j, cs=cs, ps=ps, dst=dst: e.copy(
                    dst[j][:, cs], ps[:]),
                    reads=[ps], writes=[(dst[j], f"c{c}")], tag=f"{name}_e{tg}")

    for k in range(KD):
        fw.dma("qgp", wvb[k][:], wvt[128 * k:128 * (k + 1), :], tag=f"{name}_lwv")
    for ti in range(ntt):
        fw.op("gpsimd", lambda e, ti=ti: e.memset(vo[ti][:, :], 0.0),
              writes=[(vo[ti], "z")], tag=f"{name}_vz")
    for ti in range(ntt):
        tslice = slice(128 * ti, 128 * (ti + 1))
        for dc in range(2):
            ps = pool.take()
            for k in range(KD):
                fw.op("tensor", lambda e, k=k, ts_=tslice, dc=dc, ps=ps: e.matmul(
                    ps[:], nt[k][:, ts_], wvb[k][:, 512 * dc:512 * (dc + 1)],
                    start=(k == 0), stop=(k == KD - 1)),
                    reads=[wvb[k], (nt[k], f"c{ti // 4}")], writes=[ps], tag=f"{name}_pv")
            def ev(e, ti=ti, dc=dc, ps=ps):
                dst = vo[ti][:, (hd + 1) * 8 * dc: (hd + 1) * 8 * (dc + 1)]
                dst = dst.rearrange("p (h d) -> p h d", h=8, d=hd + 1)[:, :, 0:hd]
                src = ps[:].rearrange("p (h d) -> p h d", h=8, d=hd)
                return e.tensor_copy(dst, src)
            fw.op("vector", ev, reads=[ps, (vo[ti], "z")], writes=[(vo[ti], f"d{dc}")],
                  tag=f"{name}_ev")
        def onesfn(e, ti=ti):
            dst = vo[ti][:, :].rearrange("p (h d) -> p h d", h=nheads, d=hd + 1)[:, :, hd:hd + 1]
            return e.memset(dst, 1.0)
        fw.op("gpsimd", onesfn, reads=[(vo[ti], "d0"), (vo[ti], "d1")],
              writes=[(vo[ti], "ones")], tag=f"{name}_vone")

    sqs = ctx.sqs()
    for (src, tg) in ((KT, "kb"), (QT, "qb")):
        for c in range(nc4):
            cs = slice(512 * c, 512 * (c + 1))
            ps = pool.take()
            for j in range(KD):
                sq = sqs[j % 2]
                fw.op("scalar", lambda e, j=j, cs=cs, sq=sq, src=src: e.activation(
                    sq[:], src[j][:, cs], AF.Square),
                    reads=[(src[j], f"c{c}")], writes=[sq], tag=f"{name}_sq{tg}")
                fw.op("tensor", lambda e, j=j, ps=ps, sq=sq: e.matmul(
                    ps[0:nheads, :], hsel[:, 16 * j:16 * (j + 1)], sq[:],
                    start=(j == 0), stop=(j == KD - 1)),
                    reads=[hsel, sq], writes=[ps], tag=f"{name}_ns{tg}")
            fw.op("vector", lambda e, ps=ps, cs=cs: e.tensor_copy(qn2[:, cs], ps[0:nheads, :]),
                  reads=[ps], writes=[(qn2, f"c{c}")], tag=f"{name}_nc{tg}")
        if tg == "kb":
            fw.op("vector", lambda e: e.reduce_max(kmx[:], qn2[:, :], axis=AX.X),
                  reads=[(qn2, f"c{c}") for c in range(nc4)], writes=[kmx],
                  tag=f"{name}_kmax")
            fw.op("scalar", lambda e: e.activation(kmx[:], kmx[:], AF.Sqrt),
                  reads=[kmx], writes=[kmx], tag=f"{name}_ksq")
            fw.op("vector", lambda e: e.tensor_scalar(
                kmx[:], kmx[:], -1.05, None, ALU.mult),
                reads=[kmx], writes=[kmx], tag=f"{name}_kneg")
    fw.op("scalar", lambda e: e.activation(qn2[:, :], qn2[:, :], AF.Sqrt),
          reads=[(qn2, f"c{c}") for c in range(nc4)], writes=[qn2], tag=f"{name}_qsq")
    fw.op("vector", lambda e: e.tensor_scalar(
        negc[:, :], qn2[:, :], kmx[0:nheads, :], float(margin) / float(scale),
        ALU.mult, ALU.add),
        reads=[qn2, kmx], writes=[negc], tag=f"{name}_negc")
    negc_d = fw.dram_scratch(f"{name}_negcd", [nheads, T], BF16)
    fw.dma("qsy", negc_d, negc[:, :], reads=[negc], writes=["negc_d_" + name],
           tag=f"{name}_negcs")
    negc_rows = [fw.sbuf(f"{name}_ncr{i}", [1, 512], BF16) for i in range(2)]
    ncr_i = [0]

    for c in range(nc4):
        qs = slice(512 * c, 512 * (c + 1))
        for h in range(nheads):
            jt, hh = h // HPT, h % HPT
            hp = slice(hd * hh, hd * (hh + 1))
            psO = pool.tiles[2 + (h % 2)]
            kcs = list(range(0, min(4 * c + 4, ntt) if causal else ntt))
            for i, kc in enumerate(kcs):
                psS = pool.tiles[kc % 2]
                fw.op("tensor", lambda e, jt=jt, hp=hp, kc=kc, qs=qs, psS=psS: e.matmul(
                    psS[:], KT[jt][hp, 128 * kc:128 * (kc + 1)], QT[jt][hp, qs],
                    start=True, stop=False),
                    reads=[(KT[jt], f"c{kc // 4}"), (QT[jt], f"c{c}")], writes=[psS],
                    tag=f"{name}_st")
                if i == 0:
                    ncr = negc_rows[ncr_i[0] % 2]
                    ncr_i[0] += 1
                    fw.dma("qgp", ncr[:], negc_d[h:h + 1, qs],
                           reads=["negc_d_" + name], writes=[ncr], tag=f"{name}_ncl")
                fw.op("tensor", lambda e, ncr=ncr, psS=psS: e.matmul(
                    psS[:], onesrow_b[:], ncr[:], start=False, stop=True),
                    reads=[onesrow_b, ncr], writes=[psS], tag=f"{name}_stc")
                pt = PT[kc % 2]
                sc = float(scale)
                fw.op("scalar", lambda e, psS=psS, pt=pt, sc=sc: e.activation(
                    pt[:], psS[:], AF.Exp, scale=sc),
                    reads=[psS], writes=[pt], tag=f"{name}_exp")
                if causal and kc >= 4 * c:
                    m = kc - 4 * c
                    fw.op("gpsimd", lambda e, pt=pt, m=m: e.affine_select(
                        pt[:], pt[:], pattern=[[1, 512]], compare_op=ALU.is_ge,
                        fill=0.0, base=-128 * m, channel_multiplier=-1),
                        reads=[pt], writes=[pt], tag=f"{name}_mask")
                fw.op("tensor", lambda e, kc=kc, h=h, pt=pt, psO=psO, i=i: e.matmul(
                    psO[0:hd + 1, :], vo[kc][:, (hd + 1) * h:(hd + 1) * (h + 1)], pt[:],
                    start=(i == 0), stop=(i == len(kcs) - 1)),
                    reads=[(vo[kc], "d0"), (vo[kc], "d1"), (vo[kc], "ones"), pt],
                    writes=[psO], tag=f"{name}_av")
            fw.op("vector", lambda e, psO=psO: e.tensor_scalar(
                lrow[hd:hd + 1, :], psO[hd:hd + 1, :], 1e-30, None, ALU.max),
                reads=[psO], writes=[lrow], tag=f"{name}_l")
            fw.op("vector", lambda e: e.reciprocal(rrec[hd:hd + 1, :], lrow[hd:hd + 1, :]),
                  reads=[lrow], writes=[lrow], tag=f"{name}_rec")
            psR = pool.tiles[4]
            fw.op("tensor", lambda e, psR=psR: e.matmul(
                psR[0:hd, :], ctx.ones2_f[hd:hd + 1, 0:hd], rrec[hd:hd + 1, :],
                start=True, stop=True),
                reads=[ctx.ones2_f, lrow], writes=[psR], tag=f"{name}_rb")
            fw.op("scalar", lambda e, psR=psR: e.copy(rbsb[:], psR[0:hd, :]),
                  reads=[psR], writes=[rbsb], tag=f"{name}_rbe")
            fw.op("vector", lambda e, jt=jt, hp=hp, psO=psO: e.tensor_tensor(
                otc[jt][hp, :], psO[0:hd, :], rbsb[:], ALU.mult),
                reads=[psO, rbsb], writes=[(otc[jt], f"h{hh}")], tag=f"{name}_on")
        if wop is None:
            for j in range(KD):
                xr = xres[j % 2]
                fw.dma("qsy", xr[:], xdram[128 * j:128 * (j + 1), qs], tag=f"{name}_lxr")
                xo = xout[j % 2]
                fw.op("vector", lambda e, j=j, xr=xr, xo=xo: e.tensor_tensor(
                    xo[:], xr[:], otc[j][:, :], ALU.add),
                    reads=[xr, (otc[j], "h0"), (otc[j], "h1")], writes=[xo],
                    tag=f"{name}_radd")
                fw.dma("qsy", xodram[128 * j:128 * (j + 1), qs], xo[:],
                       reads=[xo], writes=[(xodram, f"j{j}c{c}")], tag=f"{name}_sx")
        else:
            for j2 in range(KD):
                wb = wob[j2 % 2]
                fw.dma("qgp", wb[:], wop[128 * j2:128 * (j2 + 1), :], tag=f"{name}_lwo")
                ps = pool.tiles[5]
                for j in range(KD):
                    fw.op("tensor", lambda e, wb=wb, j=j, ps=ps: e.matmul(
                        ps[:], wb[:, 128 * j:128 * (j + 1)], otc[j][:, :],
                        start=(j == 0), stop=(j == KD - 1)),
                        reads=[wb, (otc[j], "h0"), (otc[j], "h1")], writes=[ps],
                        tag=f"{name}_po")
                xr = xres[j2 % 2]
                fw.dma("qsy", xr[:], xdram[128 * j2:128 * (j2 + 1), qs], tag=f"{name}_lxr")
                xo = xout[j2 % 2]
                fw.op("vector", lambda e, j2=j2, ps=ps, xr=xr, xo=xo: e.scalar_tensor_tensor(
                    xo[:], ps[:], bo[:, j2:j2 + 1], xr[:], ALU.add, ALU.add),
                    reads=[ps, bo, xr], writes=[xo], tag=f"{name}_radd")
                fw.dma("qsy", xodram[128 * j2:128 * (j2 + 1), qs], xo[:],
                       reads=[xo], writes=[(xodram, f"j{j2}c{c}")], tag=f"{name}_sx")



def emit_l0attn(ctx, pool, *, xdram, xodram, wqp, wkp, wvt, wop, bo,
                hsel, margin, name, debug=()):
    fw = ctx.fw
    hd = 64
    nt = [fw.sbuf(f"{name}_nt{k}", [128, T], BF16) for k in range(KD)]
    emit_norm(ctx, None, nt, kind="ln", eps=1e-5, name=f"{name}_ln", pool=pool,
              xdram=xdram)

    if "dump_nt" in debug:
        ntdump = fw.dram_out(f"{name}_ntdump", [KD * 128, T], F32)
        for k in range(KD):
            fw.dma("qgp", ntdump[128 * k:128 * (k + 1), :], nt[k][:, :],
                   reads=[(nt[k], f"c{c}") for c in range(4)], writes=["ntdump"],
                   tag="dmpnt")
    QT = [fw.sbuf(f"{name}_q{j}", [128, T], BF16) for j in range(KD)]
    KT = [fw.sbuf(f"{name}_k{j}", [128, T], BF16) for j in range(KD)]
    vo = [fw.sbuf(f"{name}_vo{i}", [128, 16 * (hd + 1)], BF16) for i in range(T // 128)]
    wqb = [fw.sbuf(f"{name}_wqb{i}", [128, 1024], BF16) for i in range(2)]
    wvb = [fw.sbuf(f"{name}_wvb{k}", [128, 1024], BF16) for k in range(KD)]
    PT = [fw.sbuf(f"{name}_pt{i}", [128, 512], BF16) for i in range(2)]
    otc = [fw.sbuf(f"{name}_otc{j}", [128, 512], BF16) for j in range(KD)]
    qn2 = fw.sbuf(f"{name}_qn2", [16, T], BF16)
    negc = qn2
    kmx = fw.sbuf(f"{name}_kmx", [16, 1], F32)
    kmxb = fw.sbuf(f"{name}_kmxb", [16, 1], F32)
    kmrow = fw.sbuf(f"{name}_kmrow", [1, 16], F32)
    lrow = fw.sbuf(f"{name}_lrow", [65, 512], F32)
    rrec = lrow
    rbsb = fw.sbuf(f"{name}_rbsb", [64, 512], F32)
    onesrow_b = fw.sbuf(f"{name}_or_b", [1, 128], BF16)
    fw.op("gpsimd", lambda e: e.memset(onesrow_b[:], 1.0), writes=[onesrow_b])
    xres = ctx.xstream()[0:2]
    xout = [fw.sbuf(f"{name}_xo{i}", [128, 512], F32) for i in range(2)]
    wob = wqb
    nc4 = T // 512
    ntt = T // 128

    for (wsrc, dst, tg) in ((wqp, QT, "q"), (wkp, KT, "k")):
        for j in range(KD):
            wb = wqb[j % 2]
            fw.dma("qgp", wb[:], wsrc[128 * j:128 * (j + 1), :], tag=f"{name}_lw{tg}")
            for c in range(nc4):
                cs = slice(512 * c, 512 * (c + 1))
                ps = pool.take()
                for k in range(KD):
                    fw.op("tensor", lambda e, wb=wb, k=k, cs=cs, ps=ps: e.matmul(
                        ps[:], wb[:, 128 * k:128 * (k + 1)], nt[k][:, cs],
                        start=(k == 0), stop=(k == KD - 1)),
                        reads=[wb, (nt[k], f"c{c}")], writes=[ps], tag=f"{name}_p{tg}")
                fw.op("scalar", lambda e, j=j, cs=cs, ps=ps, dst=dst: e.copy(
                    dst[j][:, cs], ps[:]),
                    reads=[ps], writes=[(dst[j], f"c{c}")], tag=f"{name}_e{tg}")

    for k in range(KD):
        fw.dma("qgp", wvb[k][:], wvt[128 * k:128 * (k + 1), :], tag=f"{name}_lwv")
    for ti in range(ntt):
        fw.op("gpsimd", lambda e, ti=ti: e.memset(vo[ti][:, :], 0.0),
              writes=[(vo[ti], "z")], tag=f"{name}_vz")
    for ti in range(ntt):
        tslice = slice(128 * ti, 128 * (ti + 1))
        for dc in range(2):
            ps = pool.take()
            for k in range(KD):
                fw.op("tensor", lambda e, k=k, ts_=tslice, dc=dc, ps=ps: e.matmul(
                    ps[:], nt[k][:, ts_], wvb[k][:, 512 * dc:512 * (dc + 1)],
                    start=(k == 0), stop=(k == KD - 1)),
                    reads=[wvb[k], (nt[k], f"c{ti // 4}")], writes=[ps], tag=f"{name}_pv")
            def ev(e, ti=ti, dc=dc, ps=ps):
                dst = vo[ti][:, (hd + 1) * 8 * dc: (hd + 1) * 8 * (dc + 1)]
                dst = dst.rearrange("p (h d) -> p h d", h=8, d=hd + 1)[:, :, 0:hd]
                src = ps[:].rearrange("p (h d) -> p h d", h=8, d=hd)
                return e.tensor_copy(dst, src)
            fw.op("vector", ev, reads=[ps, (vo[ti], "z")], writes=[(vo[ti], f"d{dc}")],
                  tag=f"{name}_ev")
        def onesfn(e, ti=ti):
            dst = vo[ti][:, :].rearrange("p (h d) -> p h d", h=16, d=hd + 1)[:, :, hd:hd + 1]
            return e.memset(dst, 1.0)
        fw.op("gpsimd", onesfn, reads=[(vo[ti], "d0"), (vo[ti], "d1")],
              writes=[(vo[ti], "ones")], tag=f"{name}_vone")

    if "dump_qk" in debug:
        qkdump = fw.dram_out(f"{name}_qkdump", [4 * 128, T], F32)
        for (ti, tens) in ((0, QT[0]), (1, QT[1]), (2, KT[0]), (3, KT[1])):
            fw.dma("qgp", qkdump[128 * ti:128 * (ti + 1), :], tens[:, :],
                   reads=[(tens, f"c{c}") for c in range(4)], writes=["qkdump"],
                   tag="dmpqk")
    sqs = ctx.sqs()
    for (src, tg) in ((KT, "kb"), (QT, "qb")):
        for c in range(nc4):
            cs = slice(512 * c, 512 * (c + 1))
            ps = pool.take()
            for j in range(KD):
                sq = sqs[j % 2]
                fw.op("scalar", lambda e, j=j, cs=cs, sq=sq, src=src: e.activation(
                    sq[:], src[j][:, cs], AF.Square),
                    reads=[(src[j], f"c{c}")], writes=[sq], tag=f"{name}_sq{tg}")
                fw.op("tensor", lambda e, j=j, ps=ps, sq=sq: e.matmul(
                    ps[0:16, :], hsel[:, 16 * j:16 * (j + 1)], sq[:],
                    start=(j == 0), stop=(j == KD - 1)),
                    reads=[hsel, sq], writes=[ps], tag=f"{name}_ns{tg}")
            fw.op("vector", lambda e, ps=ps, cs=cs: e.tensor_copy(qn2[:, cs], ps[0:16, :]),
                  reads=[ps], writes=[(qn2, f"c{c}")], tag=f"{name}_nc{tg}")
        if tg == "kb":
            fw.op("vector", lambda e: e.reduce_max(kmx[:], qn2[:, :], axis=AX.X),
                  reads=[(qn2, f"c{c}") for c in range(nc4)], writes=[kmx],
                  tag=f"{name}_kmax")
            kmx_d = fw.dram_scratch(f"{name}_kmxd", [16, 1], F32)
            fw.dma("qsy", kmx_d, kmx[:, 0:1], reads=[kmx], writes=["kmx_d" + name],
                   tag=f"{name}_kms")
            fw.dma("qsy", kmrow[0:1, :], kmx_d.rearrange("a b -> b a"),
                   reads=["kmx_d" + name], writes=[kmrow], tag=f"{name}_kml")
            fw.op("vector", lambda e: e.reduce_max(kmrow[0:1, 0:1], kmrow[0:1, :], axis=AX.X),
                  reads=[kmrow], writes=[(kmrow, "m")], tag=f"{name}_km2")
            fw.op("scalar", lambda e: e.activation(kmrow[0:1, 0:1], kmrow[0:1, 0:1], AF.Sqrt),
                  reads=[(kmrow, "m")], writes=[(kmrow, "m2")], tag=f"{name}_ksq")
            fw.op("vector", lambda e: e.tensor_scalar(
                kmrow[0:1, 0:1], kmrow[0:1, 0:1], -1.05, None, ALU.mult),
                reads=[(kmrow, "m2")], writes=[(kmrow, "m3")], tag=f"{name}_kneg")
            psb = pool.take()
            fw.op("tensor", lambda e, psb=psb: e.matmul(
                psb[0:16, 0:1], ctx.ones_row_f[0:1, 0:16], kmrow[0:1, 0:1],
                start=True, stop=True),
                reads=[ctx.ones_row_f, (kmrow, "m3")], writes=[psb], tag=f"{name}_kbc")
            fw.op("vector", lambda e, psb=psb: e.tensor_copy(kmxb[:], psb[0:16, 0:1]),
                  reads=[psb], writes=[kmxb], tag=f"{name}_kbe")
    fw.op("scalar", lambda e: e.activation(qn2[:, :], qn2[:, :], AF.Sqrt),
          reads=[(qn2, f"c{c}") for c in range(nc4)], writes=[qn2], tag=f"{name}_qsq")
    fw.op("vector", lambda e: e.tensor_scalar(
        negc[:, :], qn2[:, :], kmxb[:], float(margin), ALU.mult, ALU.add),
        reads=[qn2, kmxb], writes=[(negc, "f")], tag=f"{name}_negc")
    negc_d = fw.dram_scratch(f"{name}_negcd", [16, T], BF16)
    fw.dma("qsy", negc_d, negc[:, :], reads=[(negc, "f")], writes=["negc_d_" + name],
           tag=f"{name}_negcs")
    negc_rows = [fw.sbuf(f"{name}_ncr{i}", [1, 512], BF16) for i in range(2)]
    kst = [fw.sbuf(f"{name}_kst{i}", [64, 128], BF16) for i in range(2)]
    qst = [fw.sbuf(f"{name}_qst{i}", [64, 128], BF16) for i in range(2)]
    m31 = fw.sbuf(f"{name}_m31", [128, 31 * 128], BF16)
    fw.op("gpsimd", lambda e: e.memset(m31[:, :], 1.0), writes=[(m31, "a")],
          tag=f"{name}_m31a")
    fw.op("gpsimd", lambda e: e.affine_select(
        m31[:, :].rearrange("p (i t) -> p i t", i=31, t=128),
        m31[:, :].rearrange("p (i t) -> p i t", i=31, t=128),
        pattern=[[1, 31], [16, 128]], compare_op=ALU.is_ge,
        fill=0.0, base=-15, channel_multiplier=-16),
        reads=[(m31, "a")], writes=[m31], tag=f"{name}_m31b")

    pso_i = [0]
    for u in range(4):
        for hb in (() if "skip_core" in debug else range(4 * u, 4 * u + 4)):
            tb = slice(128 * hb, 128 * (hb + 1))
            for c in range(4):
                psO = pool.tiles[2 + pso_i[0] % 2]
                pso_i[0] += 1
                ncr = negc_rows[(hb * 4 + c) % 2]
                fw.dma("qgp", ncr[:],
                       negc_d[4 * c:4 * (c + 1), 128 * hb:128 * (hb + 1)],
                       reads=["negc_d_" + name], writes=[ncr], tag=f"{name}_ncl")
                for sgo in (1, 3):
                    g1 = 4 * c + sgo
                    fw.op("vector", lambda e, g1=g1, tb=tb, sgo=sgo: e.tensor_copy(
                        qst[sgo // 2][:, :], QT[g1 // 2][64:128, tb]),
                        reads=[(QT[g1 // 2], f"c{hb // 4}")], writes=[qst[sgo // 2]],
                        tag=f"{name}_qst")
                g2range = range(2) if "min_core" in debug else range(16)
                for g2 in g2range:
                    psS = pool.tiles[g2 % 2]
                    if g2 % 2 == 1:
                        ks = kst[(g2 // 2) % 2]
                        fw.op("vector", lambda e, g2=g2, tb=tb, ks=ks: e.tensor_copy(
                            ks[:, :], KT[g2 // 2][64:128, tb]),
                            reads=[(KT[g2 // 2], f"c{hb // 4}")], writes=[ks],
                            tag=f"{name}_kst")
                    for sg in (() if "skip_st" in debug else range(4)):
                        g1 = 4 * c + sg
                        if g2 % 2 == 0:
                            lk = KT[g2 // 2][0:64, tb]
                            lkr = (KT[g2 // 2], f"c{hb // 4}")
                        else:
                            lk = ks[:, :]
                            lkr = ks
                        if g1 % 2 == 0:
                            rq = QT[g1 // 2][0:64, tb]
                            rqr = (QT[g1 // 2], f"c{hb // 4}")
                        else:
                            rq = qst[sg // 2][:, :]
                            rqr = qst[sg // 2]
                        fw.op("tensor", lambda e, sg=sg, psS=psS, lk=lk, rq=rq: e.matmul(
                            psS[:, 128 * sg:128 * (sg + 1)], lk, rq,
                            start=(sg == 0), stop=False, skip_group_check=True),
                            reads=[lkr, rqr], writes=[psS], tag=f"{name}_st")
                    fw.op("tensor", lambda e, ncr=ncr, psS=psS: e.matmul(
                        psS[:], onesrow_b[:], ncr[:],
                        start=("skip_st" in debug), stop=True,
                        skip_group_check=True),
                        reads=[onesrow_b, ncr], writes=[psS], tag=f"{name}_stc")
                    if "dump_s" in debug and hb == 0 and c == 0 and g2 == 0:
                        sdump = fw.dram_out(f"{name}_sdump", [128, 512], F32)
                        sdsb = xout[0]
                        fw.op("vector", lambda e, psS=psS, sdsb=sdsb: e.tensor_copy(sdsb[:], psS[:]),
                              reads=[psS], writes=[sdsb], tag="dmps")
                        fw.dma("qsy", sdump, sdsb[:], reads=[sdsb], writes=["sdump"],
                               tag="dmps2")
                    pt = PT[g2 % 2]
                    fw.op("scalar", lambda e, psS=psS, pt=pt: e.activation(
                        pt[:], psS[:], AF.Exp),
                        reads=[psS], writes=[pt], tag=f"{name}_exp")
                    if "skip_mask" not in debug:
                        boff = (4 * c - g2) + 15
                        fw.op("vector", lambda e, pt=pt, boff=boff: e.tensor_tensor(
                            pt[:], pt[:], m31[:, 128 * boff:128 * boff + 512], ALU.mult),
                            reads=[pt, m31], writes=[pt], tag=f"{name}_mask")
                    if "dump_s" in debug and hb == 0 and c == 0 and g2 == 0:
                        ptdump = fw.dram_out(f"{name}_ptdump", [128, 512], F32)
                        fw.dma("qgp", ptdump, pt[:], reads=[pt], writes=["ptdump"],
                               tag="dmpt")
                    if "skip_av" not in debug:
                        fw.op("tensor", lambda e, hb=hb, g2=g2, pt=pt, psO=psO: e.matmul(
                            psO[0:hd + 1, :], vo[hb][:, (hd + 1) * g2:(hd + 1) * (g2 + 1)],
                            pt[:], start=(g2 == 0), stop=(g2 == g2range[-1])),
                            reads=[(vo[hb], "d0"), (vo[hb], "d1"), (vo[hb], "ones"), pt],
                            writes=[psO], tag=f"{name}_av")
                if "skip_norm" in debug:
                    continue
                fw.op("vector", lambda e, psO=psO: e.tensor_scalar(
                    lrow[hd:hd + 1, :], psO[hd:hd + 1, :], 1e-30, None, ALU.max),
                    reads=[psO], writes=[lrow], tag=f"{name}_l")
                fw.op("vector", lambda e: e.reciprocal(rrec[hd:hd + 1, :], lrow[hd:hd + 1, :]),
                      reads=[lrow], writes=[lrow], tag=f"{name}_rec")
                psR = pool.tiles[4]
                fw.op("tensor", lambda e, psR=psR: e.matmul(
                    psR[0:hd, :], ctx.ones2_f[hd:hd + 1, 0:hd], rrec[hd:hd + 1, :],
                    start=True, stop=True),
                    reads=[ctx.ones2_f, lrow], writes=[psR], tag=f"{name}_rb")
                fw.op("scalar", lambda e, psR=psR: e.copy(rbsb[:], psR[0:hd, :]),
                      reads=[psR], writes=[rbsb], tag=f"{name}_rbe")
                for sg in range(4):
                    g1 = 4 * c + sg
                    jt, gh = g1 // 2, g1 % 2
                    fw.op("vector", lambda e, sg=sg, jt=jt, gh=gh, hb=hb, psO=psO: e.tensor_tensor(
                        otc[jt][64 * gh:64 * gh + 64, 128 * (hb % 4):128 * (hb % 4) + 128],
                        psO[0:hd, 128 * sg:128 * (sg + 1)],
                        rbsb[:, 128 * sg:128 * (sg + 1)], ALU.mult),
                        reads=[psO, rbsb], writes=[(otc[jt], f"p{gh}_{hb % 4}")],
                        tag=f"{name}_on")
        if "dump_otc" in debug and u == 0:
            otc_dump = fw.dram_out(f"{name}_otcdump", [KD * 128, 512], F32)
            for j in range(KD):
                fw.dma("qgp", otc_dump[128 * j:128 * (j + 1), :], otc[j][:, :],
                       reads=[(otc[j], f"p{gh}_{b4}") for gh in range(2) for b4 in range(4)],
                       writes=["otcdump"], tag="dumpo")
        qs = slice(512 * u, 512 * (u + 1))
        otc_reads = lambda j: ([] if "skip_core" in debug else
            [(otc[j], f"p{gh}_{b4}") for gh in range(2) for b4 in range(4)])
        if "skip_core" in debug:
            for j in range(KD):
                fw.op("gpsimd", lambda e, j=j: e.memset(otc[j][:, :], 0.01),
                      writes=[otc[j]], tag="dbg")
            otc_reads = lambda j: [otc[j]]
        if "skip_wo" in debug:
            for j2 in range(KD):
                xo = xout[j2 % 2]
                fw.op("vector", lambda e, j2=j2, xo=xo: e.tensor_copy(
                    xo[:], otc[j2][:, :]),
                    reads=otc_reads(j2), writes=[xo], tag="dbg2")
                fw.dma("qsy", xodram[128 * j2:128 * (j2 + 1), qs], xo[:],
                       reads=[xo], writes=[(xodram, f"j{j2}c{u}")], tag="dbg3")
            continue
        for j2 in range(KD):
            wb = wob[j2 % 2]
            fw.dma("qgp", wb[:], wop[128 * j2:128 * (j2 + 1), :], tag=f"{name}_lwo")
            ps = pool.tiles[5]
            for j in range(KD):
                fw.op("tensor", lambda e, wb=wb, j=j, ps=ps: e.matmul(
                    ps[:], wb[:, 128 * j:128 * (j + 1)], otc[j][:, :],
                    start=(j == 0), stop=(j == KD - 1)),
                    reads=[wb] + otc_reads(j), writes=[ps], tag=f"{name}_po")
            xr = xres[j2 % 2]
            fw.dma("qsy", xr[:], xdram[128 * j2:128 * (j2 + 1), qs], tag=f"{name}_lxr")
            xo = xout[j2 % 2]
            fw.op("vector", lambda e, j2=j2, ps=ps, xr=xr, xo=xo: e.scalar_tensor_tensor(
                xo[:], ps[:], bo[:, j2:j2 + 1], xr[:], ALU.add, ALU.add),
                reads=[ps, bo, xr], writes=[xo], tag=f"{name}_radd")
            fw.dma("qsy", xodram[128 * j2:128 * (j2 + 1), qs], xo[:],
                   reads=[xo], writes=[(xodram, f"j{j2}c{u}")], tag=f"{name}_sx")




def emit_top2(ctx, *, lg, wts, iota8, scr, name, renorm=True, parts=None):
    fw = ctx.fw
    m1, m2, idx1, idx2, sg = scr["m1"], scr["m2"], scr["idx1"], scr["idx2"], scr["sg"]
    oh1, oh2, t8 = scr["oh1"], scr["oh2"], scr["t8"]
    fw.op("vector", lambda e: e.reduce_max(m1[:], lg[:], axis=AX.X),
          reads=[lg], writes=[m1], tag=f"{name}_m1")
    fw.op("vector", lambda e: e.tensor_scalar(t8[:], lg[:], m1[:], None, ALU.is_lt),
          reads=[lg, m1], writes=[t8], tag=f"{name}_lt1")
    fw.op("vector", lambda e: e.scalar_tensor_tensor(
        t8[:], t8[:], 100.0, iota8[:], ALU.mult, ALU.add),
        reads=[t8, iota8], writes=[(t8, "b")], tag=f"{name}_pen1")
    fw.op("vector", lambda e: e.tensor_reduce(idx1[:], t8[:], AX.X, ALU.min),
          reads=[(t8, "b")], writes=[idx1], tag=f"{name}_i1")
    fw.op("vector", lambda e: e.tensor_scalar(oh1[:], iota8[:], idx1[:], None, ALU.is_equal),
          reads=[iota8, idx1], writes=[oh1], tag=f"{name}_oh1")
    fw.op("vector", lambda e: e.scalar_tensor_tensor(
        t8[:], oh1[:], -1e30, lg[:], ALU.mult, ALU.add),
        reads=[oh1, lg], writes=[(t8, "c")], tag=f"{name}_lg2")
    fw.op("vector", lambda e: e.reduce_max(m2[:], t8[:], axis=AX.X),
          reads=[(t8, "c")], writes=[m2], tag=f"{name}_m2")
    fw.op("vector", lambda e: e.tensor_scalar(t8[:], t8[:], m2[:], None, ALU.is_lt),
          reads=[(t8, "c"), m2], writes=[(t8, "d")], tag=f"{name}_lt2")
    fw.op("vector", lambda e: e.scalar_tensor_tensor(
        t8[:], t8[:], 100.0, iota8[:], ALU.mult, ALU.add),
        reads=[(t8, "d"), iota8], writes=[(t8, "e")], tag=f"{name}_pen2")
    fw.op("vector", lambda e: e.tensor_reduce(idx2[:], t8[:], AX.X, ALU.min),
          reads=[(t8, "e")], writes=[idx2], tag=f"{name}_i2")
    fw.op("vector", lambda e: e.tensor_scalar(oh2[:], iota8[:], idx2[:], None, ALU.is_equal),
          reads=[iota8, idx2], writes=[oh2], tag=f"{name}_oh2")
    if renorm:
        fw.op("vector", lambda e: e.tensor_tensor(sg[:], m1[:], m2[:], ALU.subtract),
              reads=[m1, m2], writes=[sg], tag=f"{name}_d12")
        fw.op("scalar", lambda e: e.activation(sg[:], sg[:], AF.Sigmoid),
              reads=[sg], writes=[(sg, "s")], tag=f"{name}_sig")
        pA = t8 if parts is None else parts[0]
        pB = wts if parts is None else parts[1]
        fw.op("vector", lambda e, pA=pA: e.tensor_scalar(pA[:], oh1[:], sg[:], None, ALU.mult),
              reads=[oh1, (sg, "s")], writes=[(pA, "f")], tag=f"{name}_w1")
        fw.op("vector", lambda e: e.tensor_scalar(
            m2[:], sg[:], -1.0, 1.0, ALU.mult, ALU.add),
            reads=[(sg, "s")], writes=[(m2, "wB")], tag=f"{name}_w2")
        fw.op("vector", lambda e, pB=pB: e.tensor_scalar(
            pB[:], oh2[:], m2[:], None, ALU.mult),
            reads=[oh2, (m2, "wB")], writes=[(pB, "b")], tag=f"{name}_w3")
        fw.op("vector", lambda e, pA=pA, pB=pB: e.tensor_tensor(wts[:], pB[:], pA[:], ALU.add),
              reads=[(pB, "b"), (pA, "f")], writes=[wts], tag=f"{name}_w4")



def emit_l3ffn(ctx, pool, *, xdram, xodram, gwt, w1p, w3p, w2p, name, debug=()):
    fw = ctx.fw
    ctx.build_eye()
    HT = 1024
    ntq = [fw.sbuf(f"{name}_nt{k}", [128, HT], BF16) for k in range(KD)]
    uA = [fw.sbuf(f"{name}_uA{j}", [128, HT], BF16) for j in range(32)]
    xacc = [fw.sbuf(f"{name}_xa{j}", [128, HT], F32) for j in range(KD)]
    w13buf = [fw.sbuf(f"{name}_w13_{i}", [128, 1024], BF16) for i in range(4)]
    w2buf = [fw.sbuf(f"{name}_w2_{i}", [128, 4096], BF16) for i in range(2)]
    silu_t = [fw.sbuf(f"{name}_silu{i}", [128, 512], F32) for i in range(2)]
    gwb = fw.sbuf(f"{name}_gwb", [128, 64], BF16)
    iota8 = fw.sbuf(f"{name}_io8", [128, 8], F32)
    lg = fw.sbuf(f"{name}_lg", [128, 8], F32)
    wts = [fw.sbuf(f"{name}_wts{i}", [128, 8], F32) for i in range(8)]
    wrow = fw.sbuf(f"{name}_wrow", [1, 512], F32)
    wbsb = [fw.sbuf(f"{name}_wbsb{i}", [128, 512], F32) for i in range(2)]
    tmp = [fw.sbuf(f"{name}_tmp{i}", [128, 512], F32) for i in range(2)]
    xres = ctx.xstream()[0:2]
    xout = [fw.sbuf(f"{name}_xo{i}", [128, 512], F32) for i in range(2)]
    scr = {k: fw.sbuf(f"{name}_s{k}", [128, 1], F32) for k in ("m1", "m2", "idx1", "idx2", "sg")}
    scr["oh1"] = fw.sbuf(f"{name}_soh1", [128, 8], F32)
    scr["oh2"] = fw.sbuf(f"{name}_soh2", [128, 8], F32)
    scr["t8"] = fw.sbuf(f"{name}_st8", [128, 8], F32)

    fw.op("gpsimd", lambda e: e.iota(iota8[:], pattern=[[1, 8]], base=0,
                                     channel_multiplier=0,
                                     allow_small_or_imprecise_dtypes=True),
          writes=[iota8], tag=f"{name}_iota")
    fw.dma("qgp", gwb[:, :].rearrange("p (k e) -> p k e", k=8, e=8),
           gwt.rearrange("(k p) e -> p k e", p=128),
           reads=["gwt" + name], writes=[gwb], tag=f"{name}_ldg")

    for h in range(2):
        emit_norm(ctx, None, ntq, kind="ln", eps=1e-5, name=f"{name}_ln{h}",
                  pool=pool, t=HT, xoff=HT * h, xdram=xdram)
        for ti in range(8):
            ts_ = slice(128 * ti, 128 * (ti + 1))
            ps = pool.take()
            for k in range(KD):
                fw.op("tensor", lambda e, k=k, ts_=ts_, ps=ps: e.matmul(
                    ps[0:128, 0:8], ntq[k][:, ts_], gwb[:, 8 * k:8 * (k + 1)],
                    start=(k == 0), stop=(k == KD - 1)),
                    reads=[gwb, (ntq[k], f"c{ti // 4}")], writes=[ps], tag=f"{name}_gl")
            fw.op("vector", lambda e, ps=ps: e.tensor_copy(lg[:], ps[0:128, 0:8]),
                  reads=[ps], writes=[lg], tag=f"{name}_gle")
            emit_top2(ctx, lg=lg, wts=wts[ti], iota8=iota8, scr=scr,
                      name=f"{name}_t2_{h}_{ti}")
            if "dump_wts" in debug and h == 0:
                if ti == 0:
                    wtsdump = fw.dram_out(f"{name}_wtsdump", [1024, 8], F32)
                    lgdump = fw.dram_out(f"{name}_lgdump", [1024, 8], F32)
                fw.dma("qgp", wtsdump[128 * ti:128 * (ti + 1), :], wts[ti][:, :],
                       reads=[wts[ti]], writes=["wtsd"], tag="dwts")
                fw.dma("qgp", lgdump[128 * ti:128 * (ti + 1), :], lg[:],
                       reads=[lg], writes=["lgd"], tag="dlg")
        for ex in range(8):
            for j in range(32):
                wb1 = w13buf[(2 * j) % 4]
                wb3 = w13buf[(2 * j + 1) % 4]
                fw.dma("qgp", wb1[:], w1p[ex][128 * j:128 * (j + 1), :], tag=f"{name}_lw1")
                fw.dma("qgp", wb3[:], w3p[ex][128 * j:128 * (j + 1), :], tag=f"{name}_lw3")
                for c in range(2):
                    cs = slice(512 * c, 512 * (c + 1))
                    psA = pool.take()
                    for k in range(KD):
                        fw.op("tensor", lambda e, wb1=wb1, k=k, cs=cs, psA=psA: e.matmul(
                            psA[:], wb1[:, 128 * k:128 * (k + 1)], ntq[k][:, cs],
                            start=(k == 0), stop=(k == KD - 1)),
                            reads=[wb1, (ntq[k], f"c{c}")], writes=[psA], tag=f"{name}_s1a")
                    st = silu_t[c % 2]
                    fw.op("scalar", lambda e, psA=psA, st=st: e.activation(
                        st[:], psA[:], AF.Silu),
                        reads=[psA], writes=[st], tag=f"{name}_silu")
                    psB = pool.take()
                    for k in range(KD):
                        fw.op("tensor", lambda e, wb3=wb3, k=k, cs=cs, psB=psB: e.matmul(
                            psB[:], wb3[:, 128 * k:128 * (k + 1)], ntq[k][:, cs],
                            start=(k == 0), stop=(k == KD - 1)),
                            reads=[wb3, (ntq[k], f"c{c}")], writes=[psB], tag=f"{name}_s1b")
                    fw.op("vector", lambda e, psB=psB, st=st, j=j, cs=cs: e.tensor_tensor(
                        uA[j][:, cs], psB[:], st[:], ALU.mult),
                        reads=[psB, st], writes=[(uA[j], f"c{c}")], tag=f"{name}_u")
            if "dump_u" in debug and h == 0 and ex == 0:
                udump = fw.dram_out(f"{name}_udump", [2 * 128, HT], F32)
                for (r, j) in ((0, 0), (1, 16)):
                    fw.dma("qgp", udump[128 * r:128 * (r + 1), :], uA[j][:, :],
                           reads=[(uA[j], "c0"), (uA[j], "c1")], writes=["udump"],
                           tag="du")
            for c in range(2):
                for tt in range(4):
                    psw = pool.take()
                    wcol = scr["m1"]
                    fw.op("vector", lambda e, c=c, tt=tt, wcol=wcol, ex=ex: e.tensor_copy(
                        wcol[:], wts[4 * c + tt][:, ex:ex + 1]),
                        reads=[wts[4 * c + tt]], writes=[(wcol, "d")], tag=f"{name}_wcp")
                    fw.op("tensor", lambda e, psw=psw, wcol=wcol: e.transpose(
                        psw[0:1, 0:128], wcol[:], ctx.eye_f[:]),
                        reads=[(wcol, "d"), ctx.eye_f], writes=[psw],
                        tag=f"{name}_wtr")
                    fw.op("vector", lambda e, tt=tt, psw=psw: e.tensor_scalar(
                        wrow[0:1, 128 * tt:128 * (tt + 1)], psw[0:1, 0:128],
                        0.5, None, ALU.mult),
                        reads=[psw], writes=[(wrow, f"t{tt}")], tag=f"{name}_wrs")
                psw2 = pool.take()
                fw.op("tensor", lambda e, psw2=psw2: e.matmul(
                    psw2[:], ctx.ones_row_f[:], wrow[0:1, :], start=True, stop=True),
                    reads=[ctx.ones_row_f] + [(wrow, f"t{tt}") for tt in range(4)],
                    writes=[psw2], tag=f"{name}_wbc")
                wbs = wbsb[c % 2]
                fw.op("scalar", lambda e, psw2=psw2, wbs=wbs: e.copy(wbs[:], psw2[:]),
                      reads=[psw2], writes=[wbs], tag=f"{name}_wbe")
                if "dump_wb" in debug and h == 0 and ex == 3 and c == 0:
                    wbdump = fw.dram_out(f"{name}_wbdump", [128, 512], F32)
                    fw.dma("qgp", wbdump, wbs[:], reads=[wbs], writes=["wbd"], tag="dwb")
            for j2 in range(KD):
                wb = w2buf[j2 % 2]
                fw.dma("qgp", wb[:], w2p[ex][128 * j2:128 * (j2 + 1), :],
                       tag=f"{name}_lw2")
                for c in range(2):
                    cs = slice(512 * c, 512 * (c + 1))
                    wbs = wbsb[c % 2]
                    ps = pool.take()
                    for k in range(32):
                        fw.op("tensor", lambda e, wb=wb, k=k, cs=cs, ps=ps: e.matmul(
                            ps[:], wb[:, 128 * k:128 * (k + 1)], uA[k][:, cs],
                            start=(k == 0), stop=(k == 31)),
                            reads=[wb, (uA[k], f"c{c}")], writes=[ps], tag=f"{name}_s2")
                    tm = tmp[(2 * j2 + c) % 2]
                    fw.op("vector", lambda e, ps=ps, wbs=wbs, tm=tm: e.tensor_tensor(
                        tm[:], ps[:], wbs[:], ALU.mult),
                        reads=[ps, wbs], writes=[tm], tag=f"{name}_sc")
                    if ex == 0:
                        fw.op("vector", lambda e, tm=tm, j2=j2, cs=cs: e.tensor_copy(
                            xacc[j2][:, cs], tm[:]),
                            reads=[tm], writes=[(xacc[j2], f"c{c}")], tag=f"{name}_x0")
                    else:
                        fw.op("vector", lambda e, tm=tm, j2=j2, cs=cs: e.tensor_tensor(
                            xacc[j2][:, cs], xacc[j2][:, cs], tm[:], ALU.add),
                            reads=[tm, (xacc[j2], f"c{c}")], writes=[(xacc[j2], f"c{c}")],
                            tag=f"{name}_xs")

        if "dump_xa" in debug and h == 0:
            xadump = fw.dram_out(f"{name}_xadump", [128, HT], F32)
            fw.dma("qgp", xadump, xacc[0][:, :],
                   reads=[(xacc[0], "c0"), (xacc[0], "c1")], writes=["xad"], tag="dxa")
        for j2 in range(KD):
            for c in range(2):
                cs = slice(512 * c, 512 * (c + 1))
                xr = xres[(j2 + c) % 2]
                fw.dma("qsy", xr[:], xdram[128 * j2:128 * (j2 + 1),
                                           HT * h + 512 * c:HT * h + 512 * (c + 1)],
                       tag=f"{name}_lxr")
                xo = xout[(j2 + c) % 2]
                fw.op("vector", lambda e, xr=xr, xo=xo, j2=j2, cs=cs: e.tensor_tensor(
                    xo[:], xr[:], xacc[j2][:, cs], ALU.add),
                    reads=[xr, (xacc[j2], f"c{c}")], writes=[xo], tag=f"{name}_radd")
                fw.dma("qsy", xodram[128 * j2:128 * (j2 + 1),
                                     HT * h + 512 * c:HT * h + 512 * (c + 1)], xo[:],
                       reads=[xo], writes=[(xodram, f"j{j2}h{h}c{c}")], tag=f"{name}_sx")




def emit_l1ffn_route(ctx, pool, *, xdram, rwt, rbrow_d, wts_out, pc_out, name):
    fw = ctx.fw
    psC_hold, psC2_hold = pool.tiles[4], pool.tiles[5]
    pool = pool.sub(4)
    ntq = [fw.sbuf(f"{name}_nt{k}", [128, 512], BF16) for k in range(KD)]
    gwb = fw.sbuf(f"{name}_gwb", [128, 64], BF16)
    rbrow = fw.sbuf(f"{name}_rb", [1, 8], BF16)
    iota8 = fw.sbuf(f"{name}_io8", [128, 8], F32)
    lg = fw.sbuf(f"{name}_lg", [128, 8], F32)
    wts = fw.sbuf(f"{name}_wts", [128, 8], F32)
    pA = fw.sbuf(f"{name}_pA", [128, 8], F32)
    pB = fw.sbuf(f"{name}_pB", [128, 8], F32)
    crow = fw.sbuf(f"{name}_crow", [1, 16], F32)
    scr = {k: fw.sbuf(f"{name}_s{k}", [128, 1], F32) for k in ("m1", "m2", "idx1", "idx2", "sg")}
    scr["oh1"] = fw.sbuf(f"{name}_soh1", [128, 8], F32)
    scr["oh2"] = fw.sbuf(f"{name}_soh2", [128, 8], F32)
    scr["t8"] = fw.sbuf(f"{name}_st8", [128, 8], F32)
    onesrow_b = fw.sbuf(f"{name}_orb", [1, 128], BF16)
    fw.op("gpsimd", lambda e: e.memset(onesrow_b[:], 1.0), writes=[onesrow_b])
    fw.op("gpsimd", lambda e: e.iota(iota8[:], pattern=[[1, 8]], base=0,
                                     channel_multiplier=0,
                                     allow_small_or_imprecise_dtypes=True),
          writes=[iota8], tag=f"{name}_iota")
    fw.dma("qgp", gwb[:, :].rearrange("p (k e) -> p k e", k=8, e=8),
           rwt.rearrange("(k p) e -> p k e", p=128),
           reads=["rwt" + name], writes=[gwb], tag=f"{name}_ldg")
    fw.dma("qgp", rbrow[:], rbrow_d, reads=["rb" + name], writes=[rbrow], tag=f"{name}_ldb")
    psC = psC_hold
    psC2 = psC2_hold
    for q in range(4):
        emit_norm(ctx, None, ntq, kind="ln", eps=1e-5, name=f"{name}_ln{q}",
                  pool=pool, t=512, xoff=512 * q, xdram=xdram)
        for tq in range(4):
            ti = 4 * q + tq
            ts_ = slice(128 * tq, 128 * (tq + 1))
            ps = pool.take()
            for k in range(KD):
                fw.op("tensor", lambda e, k=k, ts_=ts_, ps=ps: e.matmul(
                    ps[0:128, 0:8], ntq[k][:, ts_], gwb[:, 8 * k:8 * (k + 1)],
                    start=(k == 0), stop=False, skip_group_check=True),
                    reads=[gwb, (ntq[k], "c0")], writes=[ps], tag=f"{name}_gl")
            fw.op("tensor", lambda e, ps=ps: e.matmul(
                ps[0:128, 0:8], onesrow_b[:], rbrow[:], start=False, stop=True,
                skip_group_check=True),
                reads=[onesrow_b, rbrow], writes=[ps], tag=f"{name}_glb")
            fw.op("vector", lambda e, ps=ps: e.tensor_copy(lg[:], ps[0:128, 0:8]),
                  reads=[ps], writes=[lg], tag=f"{name}_gle")
            emit_top2(ctx, lg=lg, wts=wts, iota8=iota8, scr=scr,
                      name=f"{name}_t2_{ti}", parts=(pA, pB))
            fw.dma("qsy", wts_out[128 * ti:128 * (ti + 1), :], wts[:],
                   reads=[wts], writes=[(wts_out, f"t{ti}")], tag=f"{name}_swts")
            for (psx, src) in ((psC, pA), (psC2, pB)):
                fw.op("tensor", lambda e, psx=psx, src=src, ti=ti: e.matmul(
                    psx[0:1, 0:8], ctx.ones_col_f[:], src[:],
                    start=(ti == 0), stop=(ti == 15)),
                    reads=[ctx.ones_col_f, (src, "f"), (src, "b")], writes=[psx],
                    tag=f"{name}_cs")
    fw.op("vector", lambda e: e.tensor_copy(
        crow[0:1, :].rearrange("o (r k) -> o k r", k=2, r=8)[:, 0:1, :], psC[0:1, 0:8]),
        reads=[psC], writes=[(crow, "a")], tag=f"{name}_c0")
    fw.op("vector", lambda e: e.tensor_copy(
        crow[0:1, :].rearrange("o (r k) -> o k r", k=2, r=8)[:, 1:2, :], psC2[0:1, 0:8]),
        reads=[psC2], writes=[(crow, "b")], tag=f"{name}_c1")
    fw.dma("qsy", pc_out, crow[0:1, :],
           reads=[(crow, "a"), (crow, "b")], writes=["pcout" + name], tag=f"{name}_spc")


def emit_l1ffn_ffn(ctx, pool, *, xdram, xodram, wts_in, allC, coreflag, w1p, w2p,
                   b1c, b2c, name, debug=()):
    fw = ctx.fw
    ctx.build_eye()
    HT = 1024
    ntq = [fw.sbuf(f"{name}_nt{k}", [128, HT], BF16) for k in range(KD)]
    xin = [fw.sbuf(f"{name}_xi{k}", [128, 512], BF16) for k in range(KD)]
    hts = [fw.sbuf(f"{name}_h{j}", [128, 512], BF16) for j in range(KD)]
    xacc = [fw.sbuf(f"{name}_xa{j}", [128, HT], F32) for j in range(KD)]
    w1b = [fw.sbuf(f"{name}_w1b{i}", [128, 1024], BF16) for i in range(KD)]
    w2b = [fw.sbuf(f"{name}_w2b{i}", [128, 1024], BF16) for i in range(KD)]
    b1sb = fw.sbuf(f"{name}_b1", [128, 64], F32)
    b2sb = fw.sbuf(f"{name}_b2", [128, 64], F32)
    wts = [fw.sbuf(f"{name}_wts{i}", [128, 8], F32) for i in range(8)]
    corr = fw.sbuf(f"{name}_corr", [128, 8], F32)
    acsb = fw.sbuf(f"{name}_ac", [8, 16], F32)
    crow = fw.sbuf(f"{name}_cr", [1, 16], F32)
    flag = fw.sbuf(f"{name}_fl", [1, 1], F32)
    wrow = fw.sbuf(f"{name}_wrow", [1, 512], F32)
    wbsb = [fw.sbuf(f"{name}_wbsb{i}", [128, 512], F32) for i in range(2)]
    tmp = [fw.sbuf(f"{name}_tmp{i}", [128, 512], F32) for i in range(2)]
    wcol = fw.sbuf(f"{name}_wcol", [128, 1], F32)
    xres = ctx.xstream()[0:2]
    xout = [fw.sbuf(f"{name}_xo{i}", [128, 512], F32) for i in range(2)]
    cd = fw.dram_scratch(f"{name}_cd", [1, 16], F32)

    fw.dma("qsy", b1sb[:], b1c, reads=["b1c" + name], writes=[b1sb], tag=f"{name}_lb1")
    fw.dma("qsy", b2sb[:], b2c, reads=["b2c" + name], writes=[b2sb], tag=f"{name}_lb2")
    fw.dma("qsy", acsb[:], allC, reads=["allC" + name], writes=[acsb], tag=f"{name}_lac")
    fw.dma("qsy", flag[:], coreflag, reads=["cf" + name], writes=[flag], tag=f"{name}_lfl")
    for ti in range(16):
        fw.dma("qsy", wts[ti % 8][:] if False else wts[ti // 2][:, :], None) if False else None
    del wts
    wts = [fw.sbuf(f"{name}_wtt{i}", [128, 8], F32) for i in range(16)]
    for ti in range(16):
        fw.dma("qsy", wts[ti][:], wts_in[128 * ti:128 * (ti + 1), :],
               reads=[(wts_in, f"t{ti}")], writes=[wts[ti]], tag=f"{name}_lwt")
    psc = pool.take()
    fw.op("tensor", lambda e, psc=psc: e.matmul(
        psc[0:1, 0:16], ctx.ones_col_f[0:8, :], acsb[:, :], start=True, stop=True),
        reads=[ctx.ones_col_f, acsb], writes=[psc], tag=f"{name}_csum")
    fw.op("vector", lambda e, psc=psc: e.tensor_scalar(
        crow[0:1, :], psc[0:1, 0:16], flag[0:1, 0:1], None, ALU.mult),
        reads=[psc, flag], writes=[crow], tag=f"{name}_cfl")
    fw.dma("qsy", cd, crow[0:1, :], reads=[crow], writes=["cd" + name], tag=f"{name}_cst")
    fw.op("gpsimd", lambda e: e.memset(corr[:, :], 0.0), writes=[(corr, "z")],
          tag=f"{name}_cz")
    fw.dma("qsy", corr[0:8, 0:2], cd.rearrange("o (r k) -> (o r) k", r=8, k=2),
           reads=["cd" + name, (corr, "z")], writes=[corr], tag=f"{name}_cld")
    fw.op("vector", lambda e: e.tensor_tensor(wts[0][:, :], wts[0][:, :], corr[:, :],
                                              ALU.add),
          reads=[wts[0], corr], writes=[(wts[0], "corr")], tag=f"{name}_cadd")
    for ti in range(16):
        rd = [(wts[0], "corr")] if ti == 0 else [wts[ti]]
        fw.op("vector", lambda e, ti=ti: e.tensor_scalar(
            wts[ti][:, :], wts[ti][:, :], 2560.0, None, ALU.min),
            reads=rd, writes=[(wts[ti], "cl")], tag=f"{name}_clamp")

    for h in range(2):
        emit_norm(ctx, None, ntq, kind="ln", eps=1e-5, name=f"{name}_ln{h}",
                  pool=pool, t=HT, xoff=HT * h, xdram=xdram)
        for ex in range(8):
            for k in range(KD):
                fw.dma("qgp", w1b[k][:], w1p[ex][128 * k:128 * (k + 1), :],
                       tag=f"{name}_lw1")
                fw.dma("qgp", w2b[k][:], w2p[ex][128 * k:128 * (k + 1), :],
                       tag=f"{name}_lw2")
            for c in range(2):
                cs = slice(512 * c, 512 * (c + 1))
                for tt in range(4):
                    gt = 8 * h + 4 * c + tt
                    psw = pool.take()
                    fw.op("vector", lambda e, gt=gt, ex=ex: e.tensor_copy(
                        wcol[:], wts[gt][:, ex:ex + 1]),
                        reads=[(wts[gt], "cl")], writes=[(wcol, "d")], tag=f"{name}_wcp")
                    fw.op("tensor", lambda e, psw=psw: e.transpose(
                        psw[0:1, 0:128], wcol[:], ctx.eye_f[:]),
                        reads=[(wcol, "d"), ctx.eye_f], writes=[psw], tag=f"{name}_wtr")
                    fw.op("vector", lambda e, tt=tt, psw=psw: e.tensor_copy(
                        wrow[0:1, 128 * tt:128 * (tt + 1)], psw[0:1, 0:128]),
                        reads=[psw], writes=[(wrow, f"t{tt}")], tag=f"{name}_wrs")
                psw2 = pool.take()
                fw.op("tensor", lambda e, psw2=psw2: e.matmul(
                    psw2[:], ctx.ones_row_f[:], wrow[0:1, :], start=True, stop=True),
                    reads=[ctx.ones_row_f] + [(wrow, f"t{tt}") for tt in range(4)],
                    writes=[psw2], tag=f"{name}_wbc")
                wbs = wbsb[c % 2]
                fw.op("scalar", lambda e, psw2=psw2, wbs=wbs: e.copy(wbs[:], psw2[:]),
                      reads=[psw2], writes=[wbs], tag=f"{name}_wbe")
                for k in range(KD):
                    fw.op("vector", lambda e, k=k, cs=cs, wbs=wbs: e.tensor_tensor(
                        xin[k][:], ntq[k][:, cs], wbs[:], ALU.mult),
                        reads=[(ntq[k], f"c{c}"), wbs], writes=[xin[k]], tag=f"{name}_xin")
                if "dump_xin" in debug and h == 0 and ex == 0 and c == 0:
                    xidump = fw.dram_out(f"{name}_xidump", [128, 512], F32)
                    fw.dma("qgp", xidump, xin[0][:], reads=[xin[0]], writes=["xid"],
                           tag="dxi")
                for j in range(KD):
                    ps = pool.take()
                    for k in range(KD):
                        fw.op("tensor", lambda e, j=j, k=k, ps=ps: e.matmul(
                            ps[:], w1b[j][:, 128 * k:128 * (k + 1)], xin[k][:],
                            start=(k == 0), stop=(k == KD - 1)),
                            reads=[w1b[j], xin[k]], writes=[ps], tag=f"{name}_h1")
                    fw.op("scalar", lambda e, j=j, ps=ps, ex=ex: e.activation(
                        hts[j][:], ps[:], AF.Gelu, bias=b1sb[:, 8 * ex + j:8 * ex + j + 1]),
                        reads=[ps, b1sb], writes=[hts[j]], tag=f"{name}_gelu")
                for j in range(KD):
                    ps = pool.take()
                    for k in range(KD):
                        fw.op("tensor", lambda e, j=j, k=k, ps=ps: e.matmul(
                            ps[:], w2b[j][:, 128 * k:128 * (k + 1)], hts[k][:],
                            start=(k == 0), stop=(k == KD - 1)),
                            reads=[w2b[j], hts[k]], writes=[ps], tag=f"{name}_o1")
                    tm = tmp[j % 2]
                    fw.op("vector", lambda e, j=j, ps=ps, tm=tm, ex=ex: e.tensor_scalar(
                        tm[:], ps[:], b2sb[:, 8 * ex + j:8 * ex + j + 1], None, ALU.add),
                        reads=[ps, b2sb], writes=[(tm, "a")], tag=f"{name}_b2")
                    fw.op("vector", lambda e, tm=tm, wbs=wbs: e.tensor_tensor(
                        tm[:], tm[:], wbs[:], ALU.mult),
                        reads=[(tm, "a"), wbs], writes=[tm], tag=f"{name}_ow")
                    if ex == 0:
                        fw.op("vector", lambda e, tm=tm, j=j, cs=cs: e.tensor_copy(
                            xacc[j][:, cs], tm[:]),
                            reads=[tm], writes=[(xacc[j], f"c{c}")], tag=f"{name}_x0")
                    else:
                        fw.op("vector", lambda e, tm=tm, j=j, cs=cs: e.tensor_tensor(
                            xacc[j][:, cs], xacc[j][:, cs], tm[:], ALU.add),
                            reads=[tm, (xacc[j], f"c{c}")], writes=[(xacc[j], f"c{c}")],
                            tag=f"{name}_xs")
        if "dump_xa" in debug and h == 0:
            xadump = fw.dram_out(f"{name}_xadump", [128, HT], F32)
            fw.dma("qgp", xadump, xacc[0][:, :],
                   reads=[(xacc[0], "c0"), (xacc[0], "c1")], writes=["xad"], tag="dxa")
        for j in range(KD):
            for c in range(2):
                xr = xres[(j + c) % 2]
                fw.dma("qsy", xr[:], xdram[128 * j:128 * (j + 1),
                                           HT * h + 512 * c:HT * h + 512 * (c + 1)],
                       tag=f"{name}_lxr")
                xo = xout[(j + c) % 2]
                fw.op("vector", lambda e, xr=xr, xo=xo, j=j, c=c: e.scalar_tensor_tensor(
                    xo[:], xacc[j][:, 512 * c:512 * (c + 1)], 0.5, xr[:],
                    ALU.mult, ALU.add),
                    reads=[xr, (xacc[j], f"c{c}")], writes=[xo], tag=f"{name}_radd")
                fw.dma("qsy", xodram[128 * j:128 * (j + 1),
                                     HT * h + 512 * c:HT * h + 512 * (c + 1)], xo[:],
                       reads=[xo], writes=[(xodram, f"j{j}h{h}c{c}")], tag=f"{name}_sx")




def emit_l2attn(ctx, pool, *, xdram, xodram, wqp, wkp, wvp, name, debug=()):
    fw = ctx.fw
    ctx.build_eye()
    H, HD = 16, 64
    nt = [fw.sbuf(f"{name}_nt{k}", [128, T], BF16) for k in range(KD)]
    emit_norm(ctx, None, nt, kind="ln", eps=1e-5, name=f"{name}_ln", pool=pool,
              xdram=xdram)
    wqb = [fw.sbuf(f"{name}_wqb{j}", [128, 1024], BF16) for j in range(KD)]
    wkb = [fw.sbuf(f"{name}_wkb{j}", [128, 1024], BF16) for j in range(KD)]
    wvb = [fw.sbuf(f"{name}_wvb{j}", [128, 1024], BF16) for j in range(KD)]
    for j in range(KD):
        fw.dma("qgp", wqb[j][:], wqp[128 * j:128 * (j + 1), :], tag=f"{name}_lq")
        fw.dma("qgp", wkb[j][:], wkp[128 * j:128 * (j + 1), :], tag=f"{name}_lk")
        fw.dma("qgp", wvb[j][:], wvp[128 * j:128 * (j + 1), :], tag=f"{name}_lv")
    QS = fw.sbuf(f"{name}_qs", [128, 1024], BF16)
    KS = fw.sbuf(f"{name}_ks", [128, 1024], BF16)
    Vt = fw.sbuf(f"{name}_vt", [128, 1024], BF16)
    sc1 = fw.sbuf(f"{name}_sc1", [128, 1024], F32)
    prod = fw.sbuf(f"{name}_prod", [128, 1024], F32)
    Mm = fw.sbuf(f"{name}_m", [128, 256], F32)
    oacc = fw.sbuf(f"{name}_oa", [128, 1024], BF16)
    tmpo = fw.sbuf(f"{name}_to", [128, 1024], F32)
    mxq = fw.sbuf(f"{name}_mxq", [128, 16], F32)
    smq = fw.sbuf(f"{name}_smq", [128, 16], F32)
    mxk = fw.sbuf(f"{name}_mxk", [128, 64], F32)
    smk = fw.sbuf(f"{name}_smk", [128, 64], F32)
    pstr = fw.psum(f"{name}_pstr", [128, 512], BF16)
    xres = ctx.xstream()[0:2]
    xout = [fw.sbuf(f"{name}_xo{i}", [128, 128], F32) for i in range(2)]

    def softmax_seg(src_sb, dst_sb, view, nseg, seglen, mx, sm, tg):
        sv = src_sb[:, :].rearrange("p (a b) -> p a b", a=view[0], b=view[1])
        dv = dst_sb[:, :].rearrange("p (a b) -> p a b", a=view[0], b=view[1])
        if view == (nseg, seglen):
            rv, bshape = sv, [128, nseg, seglen]
            bview = lambda st: st[:, :].rearrange("p (a o) -> p a o", o=1).to_broadcast(
                [128, nseg, seglen])
            wv = dv
        else:
            rv = src_sb[:, :].rearrange("p (a b) -> p b a", a=nseg, b=seglen)
            wv = dst_sb[:, :].rearrange("p (a b) -> p b a", a=nseg, b=seglen)
            bview = lambda st: st[:, :].rearrange("p (a o) -> p a o", o=1).to_broadcast(
                [128, seglen, nseg])
        fw.op("vector", lambda e: e.reduce_max(mx[:], rv, axis=AX.X),
              reads=[(src_sb, "p")], writes=[mx], tag=f"{tg}_mx")
        fw.op("vector", lambda e: e.tensor_tensor(
            sc1[:, :].rearrange("p (a b) -> p a b", a=rv.shape[1], b=rv.shape[2])
            if False else sc1[:, 0:1024].rearrange("p (a b) -> p a b", a=rv.shape[1], b=rv.shape[2]),
            rv, bview(mx), ALU.subtract),
            reads=[(src_sb, "p"), mx], writes=[sc1], tag=f"{tg}_sub")
        fw.op("scalar", lambda e: e.activation(sc1[:, :], sc1[:, :], AF.Exp),
              reads=[sc1], writes=[(sc1, "e")], tag=f"{tg}_exp")
        scv = sc1[:, 0:1024].rearrange("p (a b) -> p a b", a=rv.shape[1], b=rv.shape[2])
        fw.op("vector", lambda e: e.reduce_sum(sm[:], scv, axis=AX.X),
              reads=[(sc1, "e")], writes=[sm], tag=f"{tg}_sum")
        fw.op("vector", lambda e: e.reciprocal(sm[:], sm[:]),
              reads=[sm], writes=[(sm, "r")], tag=f"{tg}_rec")
        fw.op("vector", lambda e: e.tensor_tensor(wv, scv, bview(sm), ALU.mult),
              reads=[(sc1, "e"), (sm, "r")], writes=[(dst_sb, "sm")], tag=f"{tg}_div")

    for ti in range(T // 128):
        ts_ = slice(128 * ti, 128 * (ti + 1))
        for (wb, dst, tg) in ((wqb, QS, "q"), (wkb, KS, "k"), (wvb, Vt, "v")):
            for dc in range(2):
                ps = pool.take()
                for k in range(KD):
                    fw.op("tensor", lambda e, wb=wb, k=k, ts_=ts_, dc=dc, ps=ps: e.matmul(
                        ps[:], nt[k][:, ts_], wb[k][:, 512 * dc:512 * (dc + 1)],
                        start=(k == 0), stop=(k == KD - 1)),
                        reads=[wb[k], (nt[k], f"c{ti // 4}")], writes=[ps],
                        tag=f"{name}_p{tg}")
                fw.op("scalar", lambda e, dst=dst, dc=dc, ps=ps: e.copy(
                    dst[:, 512 * dc:512 * (dc + 1)], ps[:]),
                    reads=[ps], writes=[(dst, "p")], tag=f"{name}_e{tg}")
        softmax_seg(QS, QS, (H, HD), H, HD, mxq, smq, f"{name}_sq")
        softmax_seg(KS, KS, (HD, H), H, HD, mxk, smk, f"{name}_sk")
        for h in range(H):
            qv = QS[:, 64 * h:64 * (h + 1)].rearrange("p (o d) -> p o d", o=1)
            qb = qv.to_broadcast([128, H, HD])
            kv = KS[:, :].rearrange("p (a b) -> p a b", a=H, b=HD)
            pv = prod[:, :].rearrange("p (a b) -> p a b", a=H, b=HD)
            fw.op("vector", lambda e, qb=qb, kv=kv, pv=pv: e.tensor_tensor(
                pv, kv, qb, ALU.mult),
                reads=[(QS, "sm"), (KS, "sm")], writes=[prod], tag=f"{name}_mp")
            fw.op("vector", lambda e, h=h, pv=pv: e.reduce_sum(
                Mm[:, 16 * h:16 * (h + 1)], pv, axis=AX.X),
                reads=[prod], writes=[(Mm, f"h{h}")], tag=f"{name}_mr")
        for hp in range(H):
            mv = Mm[:, :].rearrange("p (a b) -> p a b", a=H, b=H)[:, :, hp:hp + 1]
            mb = mv.to_broadcast([128, H, HD])
            vv = Vt[:, 64 * hp:64 * (hp + 1)].rearrange("p (o d) -> p o d", o=1)
            vb = vv.to_broadcast([128, H, HD])
            tv = tmpo[:, :].rearrange("p (a b) -> p a b", a=H, b=HD)
            if hp == 0:
                ov = oacc[:, :].rearrange("p (a b) -> p a b", a=H, b=HD)
                fw.op("vector", lambda e, mb=mb, vb=vb, ov=ov: e.tensor_tensor(
                    ov, mb, vb, ALU.mult),
                    reads=[(Mm, f"h{h}") for h in range(H)] + [(Vt, "sm"), (Vt, "p")],
                    writes=[(oacc, "a")], tag=f"{name}_o0")
            else:
                fw.op("vector", lambda e, mb=mb, vb=vb, tv=tv: e.tensor_tensor(
                    tv, mb, vb, ALU.mult),
                    reads=[(Mm, f"h{h}") for h in range(H)] + [(Vt, "p")],
                    writes=[tmpo], tag=f"{name}_om")
                fw.op("vector", lambda e: e.tensor_tensor(
                    oacc[:, :], oacc[:, :], tmpo[:, :], ALU.add),
                    reads=[tmpo, (oacc, "a")], writes=[(oacc, "a")], tag=f"{name}_oa")
        if "dump" in debug and ti == 0:
            dd = fw.dram_out(f"{name}_dd", [4 * 128, 1024], F32)
            fw.dma("qgp", dd[0:128, :], QS[:, :], reads=[(QS, "sm")], writes=["dd0"], tag="d0")
            fw.dma("qgp", dd[128:256, :], KS[:, :], reads=[(KS, "sm")], writes=["dd1"], tag="d1")
            fw.dma("qgp", dd[256:384, 0:256], Mm[:, :],
                   reads=[(Mm, f"h{hh}") for hh in range(16)], writes=["dd2"], tag="d2")
            fw.dma("qgp", dd[384:512, :], oacc[:, :], reads=[(oacc, "a")], writes=["dd3"], tag="d3")
        for j in range(KD):
            seg = j % 4
            fw.op("tensor", lambda e, j=j, seg=seg: e.transpose(
                pstr[:, 128 * seg:128 * (seg + 1)], oacc[:, 128 * j:128 * (j + 1)],
                ctx.eye_b[:]),
                reads=[(oacc, "a"), ctx.eye_b], writes=[pstr, (pstr, f"s{seg}")],
                tag=f"{name}_tr")
            xr = xres[j % 2]
            fw.dma("qsy", xr[0:128, 0:128], xdram[128 * j:128 * (j + 1), ts_],
                   tag=f"{name}_lxr")
            xo = xout[j % 2]
            fw.op("vector", lambda e, xr=xr, xo=xo, seg=seg: e.tensor_tensor(
                xo[:], xr[0:128, 0:128], pstr[:, 128 * seg:128 * (seg + 1)], ALU.add),
                reads=[xr, (pstr, f"s{seg}"), pstr], writes=[xo, (pstr, f"rd{seg}")],
                tag=f"{name}_radd")
            fw.dma("qsy", xodram[128 * j:128 * (j + 1), ts_], xo[:],
                   reads=[xo], writes=[(xodram, f"j{j}t{ti}")], tag=f"{name}_sx")



def emit_l3attn(ctx, pool, *, xdram, xodram, qkvp, opp, ogp, ogb_c, dp_d, cwv_d,
                cb_c, pwp, gatep, w1p, w2p, w3p, n1c, n2c, name):
    fw = ctx.fw
    ctx.build_eye()
    NH, HD = 8, 128
    ht = [fw.sbuf(f"{name}_ht{k}", [128, T], BF16) for k in range(KD)]
    Oa = [fw.sbuf(f"{name}_oa{k}", [128, T], BF16) for k in range(KD)]
    qe = fw.sbuf(f"{name}_qe", [128, T], BF16)
    ke = fw.sbuf(f"{name}_ke", [128, T], BF16)
    kvb = fw.sbuf(f"{name}_kv", [128, T], BF16)
    rkv = fw.sbuf(f"{name}_rkv", [128, T], BF16)
    rks = fw.sbuf(f"{name}_rks", [128, T], BF16)
    n1sb = fw.sbuf(f"{name}_n1", [128, 8], F32)
    n2sb = fw.sbuf(f"{name}_n2", [128, 8], F32)
    ogbsb = fw.sbuf(f"{name}_ogb", [128, 8], F32)
    cbsb = fw.sbuf(f"{name}_cb", [128, 8], F32)
    cwsb = fw.sbuf(f"{name}_cw", [128, 32], F32)
    dpsb = fw.sbuf(f"{name}_dp", [1, 8], F32)
    logd = fw.sbuf(f"{name}_ld", [1, 8], F32)
    ldcol = fw.sbuf(f"{name}_ldc", [128, 1], F32)
    ldsc = fw.sbuf(f"{name}_lds", [1, 1], F32)
    tril = fw.sbuf(f"{name}_tril", [128, 128], F32)
    dmat = fw.sbuf(f"{name}_dm", [128, 128], F32)
    tmT = fw.sbuf(f"{name}_tmT", [128, 128], BF16)
    tmf = fw.sbuf(f"{name}_tmf", [128, 128], F32)
    sel127 = fw.sbuf(f"{name}_sel", [128, 1], BF16)
    iotar = fw.sbuf(f"{name}_ior", [1, 128], F32)
    initp = fw.sbuf(f"{name}_ip", [1, 128], BF16)
    state = fw.sbuf(f"{name}_st", [1, 256], BF16)
    stg = [fw.sbuf(f"{name}_stg{i}", [128, 256], BF16) for i in range(2)]
    wbuf = [fw.sbuf(f"{name}_wb{i}", [128, 2816], BF16) for i in range(2)]
    acc = fw.sbuf(f"{name}_acc", [128, 512], F32)
    csq = [fw.sbuf(f"{name}_csq{k}", [128, 512], BF16) for k in range(KD)]
    uA = [fw.sbuf(f"{name}_uA{j}", [128, 512], BF16) for j in range(22)]
    ntq = [fw.sbuf(f"{name}_ntq{k}", [128, 512], BF16) for k in range(KD)]
    sfix = [fw.sbuf(f"{name}_sf{i}", [128, 512], F32) for i in range(2)]
    vstr = [fw.sbuf(f"{name}_vs{i}", [128, 512], BF16) for i in range(2)]
    lstr = [fw.sbuf(f"{name}_lstr{i}", [128, 512], BF16) for i in range(2)]
    den = fw.sbuf(f"{name}_den", [1, T], F32)
    gt = fw.sbuf(f"{name}_gt", [128, 512], BF16)
    xres = ctx.xstream()[0:2]
    xout = [fw.sbuf(f"{name}_xo{i}", [128, 512], F32) for i in range(2)]
    pstr = fw.psum(f"{name}_pstr", [128, 512], BF16)
    local_d = fw.dram_scratch(f"{name}_locald", [D, T], BF16)
    v_d = fw.dram_scratch(f"{name}_vd", [D, T], BF16)
    attn_d = fw.dram_scratch(f"{name}_attnd", [D, T], BF16)

    for (sb, dr, tg) in ((n1sb, n1c, "n1"), (n2sb, n2c, "n2"), (ogbsb, ogb_c, "ogb"),
                         (cbsb, cb_c, "cb"), (cwsb, cwv_d, "cw"), (dpsb, dp_d, "dp")):
        fw.dma("qsy", sb[:], dr, reads=[f"{name}_{tg}d"], writes=[sb], tag=f"{name}_l{tg}")
    ii = fw.sbuf(f"{name}_ii", [128, 128], I32)
    fw.op("gpsimd", lambda e: e.iota(ii[:], pattern=[[1, 128]], base=0, channel_multiplier=-1),
          writes=[ii], tag=f"{name}_i1")
    fw.op("vector", lambda e: e.tensor_copy(dmat[:], ii[:]),
          reads=[ii], writes=[dmat], tag=f"{name}_i2")
    fw.op("vector", lambda e: e.tensor_scalar(tril[:], dmat[:], 0.0, None, ALU.is_ge),
          reads=[dmat], writes=[tril], tag=f"{name}_i3")
    iic = fw.sbuf(f"{name}_iic", [128, 1], I32)
    fw.op("gpsimd", lambda e: e.iota(iic[:], pattern=[[0, 1]], base=0, channel_multiplier=1),
          writes=[iic], tag=f"{name}_i4")
    fw.op("vector", lambda e: e.tensor_scalar(sel127[:], iic[:], 127.0, None, ALU.is_equal),
          reads=[iic], writes=[sel127], tag=f"{name}_i5")
    iir = fw.sbuf(f"{name}_iir", [1, 128], I32)
    fw.op("gpsimd", lambda e: e.iota(iir[:], pattern=[[1, 128]], base=1, channel_multiplier=0),
          writes=[iir], tag=f"{name}_i6")
    fw.op("vector", lambda e: e.tensor_copy(iotar[:], iir[:]),
          reads=[iir], writes=[iotar], tag=f"{name}_i7")
    fw.op("scalar", lambda e: e.activation(logd[:], dpsb[:], AF.Sigmoid),
          reads=[dpsb], writes=[(logd, "a")], tag=f"{name}_ld1")
    fw.op("vector", lambda e: e.tensor_scalar(logd[:], logd[:], 1e-8, None, ALU.max),
          reads=[(logd, "a")], writes=[(logd, "b")], tag=f"{name}_ld2")
    fw.op("scalar", lambda e: e.activation(logd[:], logd[:], AF.Ln),
          reads=[(logd, "b")], writes=[logd], tag=f"{name}_ld3")

    emit_norm(ctx, None, ht, kind="rms", eps=1e-6, name=f"{name}_rn1", pool=pool,
              xdram=xdram, gamma=[n1sb[:, k:k + 1] for k in range(KD)])

    for q in range(4):
        for k in range(KD):
            fw.op("gpsimd", lambda e: e.memset(acc[:], 0.0), writes=[acc],
                  tag=f"{name}_cz")
            for i in range(4):
                sh = 3 - i
                lo = 512 * q - sh
                if lo >= 0:
                    fw.op("vector", lambda e, k=k, i=i, lo=lo: e.scalar_tensor_tensor(
                        acc[:, 0:512], ht[k][:, lo:lo + 512],
                        cwsb[:, 4 * k + i:4 * k + i + 1], acc[:, 0:512],
                        ALU.mult, ALU.add),
                        reads=[ht[k], cwsb, acc], writes=[acc], tag=f"{name}_cv")
                else:
                    fw.op("vector", lambda e, k=k, i=i, sh=sh: e.scalar_tensor_tensor(
                        acc[:, sh:512], ht[k][:, 0:512 - sh],
                        cwsb[:, 4 * k + i:4 * k + i + 1], acc[:, sh:512],
                        ALU.mult, ALU.add),
                        reads=[ht[k], cwsb, acc], writes=[acc], tag=f"{name}_cve")
            fw.op("scalar", lambda e, k=k: e.activation(
                csq[k][:], acc[:], AF.Silu, bias=cbsb[:, k:k + 1]),
                reads=[acc, cbsb], writes=[csq[k]], tag=f"{name}_csilu")
        for j in range(KD):
            wb = wbuf[j % 2]
            fw.dma("qgp", wb[:, 0:1024], pwp[128 * j:128 * (j + 1), :], tag=f"{name}_lpw")
            ps = pool.take()
            for k in range(KD):
                fw.op("tensor", lambda e, wb=wb, k=k, ps=ps: e.matmul(
                    ps[:], wb[:, 128 * k:128 * (k + 1)], csq[k][:],
                    start=(k == 0), stop=(k == KD - 1)),
                    reads=[wb, csq[k]], writes=[ps], tag=f"{name}_pl")
            sf = lstr[j % 2]
            fw.op("vector", lambda e, ps=ps, sf=sf: e.tensor_copy(sf[:], ps[:]),
                  reads=[ps], writes=[sf], tag=f"{name}_le")
            fw.dma("qgp", local_d[128 * j:128 * (j + 1), 512 * q:512 * (q + 1)], sf[:],
                   reads=[sf], writes=[(local_d, f"j{j}q{q}")], tag=f"{name}_lsv")

    elu_t = fw.sbuf(f"{name}_elu", [128, 512], F32)
    elu_t2 = fw.sbuf(f"{name}_elu2", [128, 512], F32)
    for h in range(NH):
        fw.op("tensor", lambda e, h=h: e.matmul(
            pool.tiles[5][0:128, 0:1], ctx.ones_row_f[:], logd[0:1, h:h + 1],
            start=True, stop=True),
            reads=[ctx.ones_row_f, logd], writes=[pool.tiles[5]], tag=f"{name}_ldb")
        fw.op("vector", lambda e: e.tensor_copy(ldcol[:], pool.tiles[5][0:128, 0:1]),
              reads=[pool.tiles[5]], writes=[ldcol], tag=f"{name}_ldbe")
        fw.op("scalar", lambda e: e.activation(tmf[:], dmat[:], AF.Exp, scale=ldcol[:]),
              reads=[dmat, ldcol], writes=[tmf], tag=f"{name}_tm1")
        fw.op("vector", lambda e: e.tensor_tensor(tmT[:], tmf[:], tril[:], ALU.mult),
              reads=[tmf, tril], writes=[tmT], tag=f"{name}_tm2")
        fw.op("scalar", lambda e, h=h: e.activation(
            initp[:], iotar[:], AF.Exp, scale=logd[0:1, h:h + 1]),
            reads=[iotar, logd], writes=[initp], tag=f"{name}_ip1")
        for (row, tg) in ((h, "q"), (8 + h, "k"), (16 + h, "v")):
            wb = wbuf[0]
            fw.dma("qgp", wb[:, 0:1024], qkvp[128 * row:128 * (row + 1), :],
                   tag=f"{name}_lqkv")
            for c in range(4):
                cs = slice(512 * c, 512 * (c + 1))
                ps = pool.take()
                for k in range(KD):
                    fw.op("tensor", lambda e, wb=wb, k=k, cs=cs, ps=ps: e.matmul(
                        ps[:], wb[:, 128 * k:128 * (k + 1)], ht[k][:, cs],
                        start=(k == 0), stop=(k == KD - 1)),
                        reads=[wb, (ht[k], f"c{c}")], writes=[ps], tag=f"{name}_pq"),
                if tg == "v":
                    sf = vstr[c % 2]
                    fw.op("vector", lambda e, ps=ps, sf=sf: e.tensor_copy(sf[:], ps[:]),
                          reads=[ps], writes=[sf], tag=f"{name}_ve")
                    fw.dma("qgp", v_d[128 * h:128 * (h + 1), cs], sf[:],
                           reads=[sf], writes=[(v_d, f"h{h}c{c}")], tag=f"{name}_vs")
                    fw.op("vector", lambda e, ps=ps, cs=cs: e.tensor_tensor(
                        kvb[:, cs], ps[:], ke[:, cs], ALU.mult),
                        reads=[ps, (ke, f"c{c}")], writes=[(kvb, f"c{c}")],
                        tag=f"{name}_kv")
                else:
                    dst = qe if tg == "q" else ke
                    fw.op("vector", lambda e, ps=ps: e.tensor_scalar(
                        elu_t[:], ps[:], 0.0, None, ALU.min),
                        reads=[ps], writes=[elu_t], tag=f"{name}_e1")
                    fw.op("vector", lambda e, ps=ps: e.tensor_scalar(
                        elu_t2[:], ps[:], 0.0, None, ALU.max),
                        reads=[ps], writes=[elu_t2], tag=f"{name}_e2")
                    fw.op("scalar", lambda e: e.activation(elu_t[:], elu_t[:], AF.Exp),
                          reads=[elu_t], writes=[(elu_t, "x")], tag=f"{name}_e3")
                    fw.op("vector", lambda e, dst=dst, cs=cs: e.tensor_tensor(
                        dst[:, cs], elu_t[:], elu_t2[:], ALU.add),
                        reads=[(elu_t, "x"), elu_t2], writes=[(dst, f"c{c}")],
                        tag=f"{name}_e4")
        fw.op("gpsimd", lambda e: e.memset(state[:], 0.0), writes=[state],
              tag=f"{name}_st0")
        for cchunk in range(T // 128):
            tb = slice(128 * cchunk, 128 * (cchunk + 1))
            st0 = stg[0]
            fw.op("tensor", lambda e, tb=tb: e.transpose(
                pstr[:, 0:128], kvb[:, tb], ctx.eye_b[:]),
                reads=[(kvb, f"c{tb.start // 512}")], writes=[(pstr, "a"), pstr],
                tag=f"{name}_t1")
            fw.op("tensor", lambda e, tb=tb: e.transpose(
                pstr[:, 128:256], ke[:, tb], ctx.eye_b[:]),
                reads=[(ke, f"c{tb.start // 512}")], writes=[(pstr, "b"), pstr],
                tag=f"{name}_t2")
            fw.op("vector", lambda e, st0=st0: e.tensor_copy(st0[:], pstr[:, 0:256]),
                  reads=[(pstr, "a"), (pstr, "b"), pstr], writes=[st0], tag=f"{name}_t3")
            ps = pool.take()
            fw.op("tensor", lambda e, ps=ps, st0=st0: e.matmul(
                ps[:, 0:256], tmT[:], st0[:], start=True, stop=False),
                reads=[tmT, st0], writes=[ps], tag=f"{name}_sc1")
            fw.op("tensor", lambda e, ps=ps: e.matmul(
                ps[:, 0:256], initp[:], state[:], start=False, stop=True),
                reads=[initp, state], writes=[ps], tag=f"{name}_sc2")
            st1 = stg[1]
            fw.op("vector", lambda e, ps=ps, st1=st1: e.tensor_copy(st1[:], ps[:, 0:256]),
                  reads=[ps], writes=[st1], tag=f"{name}_sc3")
            ps2 = pool.take()
            fw.op("tensor", lambda e, ps2=ps2, st1=st1: e.matmul(
                ps2[0:1, 0:256], sel127[:], st1[:], start=True, stop=True),
                reads=[sel127, st1], writes=[ps2], tag=f"{name}_sc4")
            fw.op("scalar", lambda e, ps2=ps2: e.copy(state[:], ps2[0:1, 0:256]),
                  reads=[ps2], writes=[state], tag=f"{name}_sc5")
            fw.op("tensor", lambda e, st1=st1: e.transpose(
                pstr[:, 256:384], st1[:, 0:128], ctx.eye_b[:]),
                reads=[st1, pstr], writes=[(pstr, "c"), pstr], tag=f"{name}_t4")
            fw.op("tensor", lambda e, st1=st1: e.transpose(
                pstr[:, 384:512], st1[:, 128:256], ctx.eye_b[:]),
                reads=[st1, pstr], writes=[(pstr, "d"), pstr], tag=f"{name}_t5")
            fw.op("vector", lambda e, tb=tb: e.tensor_copy(rkv[:, tb], pstr[:, 256:384]),
                  reads=[(pstr, "c"), pstr], writes=[(rkv, f"cc{cchunk}")], tag=f"{name}_t6")
            fw.op("vector", lambda e, tb=tb: e.tensor_copy(rks[:, tb], pstr[:, 384:512]),
                  reads=[(pstr, "d"), pstr], writes=[(rks, f"cc{cchunk}")], tag=f"{name}_t7")
        for c in range(4):
            cs = slice(512 * c, 512 * (c + 1))
            rdk = [(rks, f"cc{i}") for i in range(4 * c, 4 * c + 4)]
            fw.op("vector", lambda e, cs=cs: e.tensor_tensor(
                elu_t[:], qe[:, cs], rks[:, cs], ALU.mult),
                reads=[(qe, f"c{c}")] + rdk, writes=[elu_t], tag=f"{name}_dp1")
            ps = pool.take()
            fw.op("tensor", lambda e, ps=ps: e.matmul(
                ps[0:1, :], ctx.ones_col_f[:], elu_t[:], start=True, stop=True),
                reads=[ctx.ones_col_f, elu_t], writes=[ps], tag=f"{name}_dp2")
            fw.op("vector", lambda e, ps=ps, cs=cs: e.tensor_scalar(
                den[0:1, cs], ps[0:1, :], 1e-6, None, ALU.max),
                reads=[ps], writes=[(den, f"c{c}")], tag=f"{name}_dp3")
            fw.op("vector", lambda e, cs=cs: e.reciprocal(den[0:1, cs], den[0:1, cs]),
                  reads=[(den, f"c{c}")], writes=[(den, f"r{c}")], tag=f"{name}_dp4")
            psb = pool.take()
            fw.op("tensor", lambda e, psb=psb, cs=cs: e.matmul(
                psb[:], ctx.ones_row_f[:], den[0:1, cs], start=True, stop=True),
                reads=[ctx.ones_row_f, (den, f"r{c}")], writes=[psb], tag=f"{name}_dp5")
            rdv = [(rkv, f"cc{i}") for i in range(4 * c, 4 * c + 4)]
            fw.op("vector", lambda e, cs=cs: e.tensor_tensor(
                elu_t2[:], qe[:, cs], rkv[:, cs], ALU.mult),
                reads=[(qe, f"c{c}")] + rdv, writes=[elu_t2], tag=f"{name}_dp6")
            fw.op("vector", lambda e, h=h, cs=cs, psb=psb: e.tensor_tensor(
                Oa[h][:, cs], elu_t2[:], psb[:], ALU.mult),
                reads=[elu_t2, psb], writes=[(Oa[h], f"c{c}")], tag=f"{name}_dp7")

    for j in range(KD):
        wb = wbuf[j % 2]
        fw.dma("qgp", wb[:, 0:1024], ogp[128 * j:128 * (j + 1), :], tag=f"{name}_log")
        for c in range(4):
            cs = slice(512 * c, 512 * (c + 1))
            ps = pool.take()
            for k in range(KD):
                fw.op("tensor", lambda e, wb=wb, k=k, cs=cs, ps=ps: e.matmul(
                    ps[:], wb[:, 128 * k:128 * (k + 1)], Oa[k][:, cs],
                    start=(k == 0), stop=(k == KD - 1)),
                    reads=[wb, (Oa[k], f"c{c}")], writes=[ps], tag=f"{name}_pg")
            fw.op("scalar", lambda e, j=j, ps=ps: e.activation(
                gt[:], ps[:], AF.Sigmoid, bias=ogbsb[:, j:j + 1]),
                reads=[ps, ogbsb], writes=[gt], tag=f"{name}_sg")
            vv = vstr[c % 2]
            fw.dma("qsy", vv[:], v_d[128 * j:128 * (j + 1), cs],
                   reads=[(v_d, f"h{j}c{c}")], writes=[vv], tag=f"{name}_lv2")
            fw.op("vector", lambda e, j=j, cs=cs, vv=vv: e.tensor_tensor(
                elu_t[:], Oa[j][:, cs], vv[:], ALU.subtract),
                reads=[(Oa[j], f"c{c}"), vv], writes=[elu_t], tag=f"{name}_o21")
            fw.op("vector", lambda e: e.tensor_tensor(elu_t[:], elu_t[:], gt[:], ALU.mult),
                  reads=[elu_t, gt], writes=[(elu_t, "m")], tag=f"{name}_o22")
            fw.op("vector", lambda e, j=j, cs=cs, vv=vv: e.tensor_tensor(
                ht[j][:, cs], elu_t[:], vv[:], ALU.add),
                reads=[(elu_t, "m"), vv], writes=[(ht[j], f"o2c{c}")], tag=f"{name}_o23")
    for j in range(KD):
        wb = wbuf[j % 2]
        fw.dma("qgp", wb[:, 0:1024], opp[128 * j:128 * (j + 1), :], tag=f"{name}_lop")
        for c in range(4):
            cs = slice(512 * c, 512 * (c + 1))
            ps = pool.take()
            for k in range(KD):
                fw.op("tensor", lambda e, wb=wb, k=k, cs=cs, ps=ps: e.matmul(
                    ps[:], wb[:, 128 * k:128 * (k + 1)], ht[k][:, cs],
                    start=(k == 0), stop=(k == KD - 1)),
                    reads=[wb, (ht[k], f"o2c{c}")], writes=[ps], tag=f"{name}_pa")
            sf = sfix[c % 2]
            fw.op("vector", lambda e, ps=ps, sf=sf: e.tensor_copy(sf[:], ps[:]),
                  reads=[ps], writes=[sf], tag=f"{name}_ae")
            fw.dma("qgp", attn_d[128 * j:128 * (j + 1), cs], sf[:],
                   reads=[sf], writes=[(attn_d, f"j{j}c{c}")], tag=f"{name}_as")
    for j in range(KD):
        wb = wbuf[j % 2]
        fw.dma("qgp", wb[:, 0:2048], gatep[128 * j:128 * (j + 1), :], tag=f"{name}_lgw")
        for c in range(4):
            cs = slice(512 * c, 512 * (c + 1))
            ps = pool.take()
            lb = lstr[0]
            ab = lstr[1]
            for k in range(KD):
                fw.dma("qsy", lb[:], local_d[128 * k:128 * (k + 1), cs],
                       reads=[(local_d, f"j{k}q{c}")], writes=[lb], tag=f"{name}_ll")
                fw.op("tensor", lambda e, wb=wb, k=k, ps=ps, lb=lb: e.matmul(
                    ps[:], wb[:, 128 * k:128 * (k + 1)], lb[:],
                    start=(k == 0), stop=False, skip_group_check=True),
                    reads=[wb, lb], writes=[ps], tag=f"{name}_pm1")
                fw.dma("qsy", ab[:], attn_d[128 * k:128 * (k + 1), cs],
                       reads=[(attn_d, f"j{k}c{c}")], writes=[ab], tag=f"{name}_la")
                fw.op("tensor", lambda e, wb=wb, k=k, ps=ps, ab=ab: e.matmul(
                    ps[:], wb[:, 128 * (8 + k):128 * (9 + k)], ab[:],
                    start=False, stop=(k == KD - 1), skip_group_check=True),
                    reads=[wb, ab], writes=[ps], tag=f"{name}_pm2")
            fw.op("scalar", lambda e, ps=ps: e.activation(gt[:], ps[:], AF.Sigmoid),
                  reads=[ps], writes=[gt], tag=f"{name}_gm")
            fw.dma("qsy", lb[:], local_d[128 * j:128 * (j + 1), cs],
                   reads=[(local_d, f"j{j}q{c}")], writes=[(lb, "x")], tag=f"{name}_ll2")
            fw.dma("qsy", ab[:], attn_d[128 * j:128 * (j + 1), cs],
                   reads=[(attn_d, f"j{j}c{c}")], writes=[(ab, "x")], tag=f"{name}_la2")
            fw.op("vector", lambda e, lb=lb, ab=ab: e.tensor_tensor(
                elu_t[:], lb[:], ab[:], ALU.subtract),
                reads=[(lb, "x"), (ab, "x")], writes=[elu_t], tag=f"{name}_mx1")
            fw.op("vector", lambda e: e.tensor_tensor(elu_t[:], elu_t[:], gt[:], ALU.mult),
                  reads=[elu_t, gt], writes=[(elu_t, "m")], tag=f"{name}_mx2")
            fw.op("vector", lambda e, ab=ab: e.tensor_tensor(
                elu_t[:], elu_t[:], ab[:], ALU.add),
                reads=[(elu_t, "m"), (ab, "x")], writes=[(elu_t, "s")], tag=f"{name}_mx3")
            xr = xres[c % 2]
            fw.dma("qsy", xr[:], xdram[128 * j:128 * (j + 1), cs], tag=f"{name}_lxr")
            xo = xout[c % 2]
            fw.op("vector", lambda e, xr=xr, xo=xo: e.tensor_tensor(
                xo[:], xr[:], elu_t[:], ALU.add),
                reads=[xr, (elu_t, "s")], writes=[xo], tag=f"{name}_mx4")
            fw.dma("qsy", xodram[128 * j:128 * (j + 1), cs], xo[:],
                   reads=[xo], writes=[(xodram, f"m{j}c{c}")], tag=f"{name}_sxm")

    for q in range(4):
        qs = slice(512 * q, 512 * (q + 1))
        emit_norm(ctx, None, ntq, kind="rms", eps=1e-6, name=f"{name}_rn2_{q}",
                  pool=pool, t=512, xoff=512 * q, xdram=xodram,
                  gamma=[n2sb[:, k:k + 1] for k in range(KD)])
        for j in range(22):
            wb1 = wbuf[0]
            wb2 = wbuf[1]
            fw.dma("qgp", wb1[:, 0:1024], w1p[128 * j:128 * (j + 1), :], tag=f"{name}_lw1")
            fw.dma("qgp", wb2[:, 0:1024], w2p[128 * j:128 * (j + 1), :], tag=f"{name}_lw2")
            psA = pool.take()
            for k in range(KD):
                fw.op("tensor", lambda e, wb1=wb1, k=k, psA=psA: e.matmul(
                    psA[:], wb1[:, 128 * k:128 * (k + 1)], ntq[k][:, 0:512],
                    start=(k == 0), stop=(k == KD - 1)),
                    reads=[wb1, (ntq[k], "c0")], writes=[psA], tag=f"{name}_f1")
            fw.op("scalar", lambda e, psA=psA: e.activation(sfix[0][:], psA[:], AF.Silu),
                  reads=[psA], writes=[sfix[0]], tag=f"{name}_fs")
            psB = pool.take()
            for k in range(KD):
                fw.op("tensor", lambda e, wb2=wb2, k=k, psB=psB: e.matmul(
                    psB[:], wb2[:, 128 * k:128 * (k + 1)], ntq[k][:, 0:512],
                    start=(k == 0), stop=(k == KD - 1)),
                    reads=[wb2, (ntq[k], "c0")], writes=[psB], tag=f"{name}_f2")
            fw.op("vector", lambda e, j=j, psB=psB: e.tensor_tensor(
                uA[j][:], psB[:], sfix[0][:], ALU.mult),
                reads=[psB, sfix[0]], writes=[uA[j]], tag=f"{name}_fu")
        for j2 in range(KD):
            wb = wbuf[j2 % 2]
            fw.dma("qgp", wb[:, 0:(22 * 128)], w3p[128 * j2:128 * (j2 + 1), :],
                   tag=f"{name}_lw3")
            ps = pool.take()
            for k in range(22):
                fw.op("tensor", lambda e, wb=wb, k=k, ps=ps: e.matmul(
                    ps[:], wb[:, 128 * k:128 * (k + 1)], uA[k][:],
                    start=(k == 0), stop=(k == 21)),
                    reads=[wb, uA[k]], writes=[ps], tag=f"{name}_f3")
            xr = xres[j2 % 2]
            fw.dma("qsy", xr[:], xodram[128 * j2:128 * (j2 + 1), qs],
                   reads=[(xodram, f"m{j2}c{q}")], writes=[xr], tag=f"{name}_lxf")
            xo = xout[j2 % 2]
            fw.op("vector", lambda e, ps=ps, xr=xr, xo=xo: e.tensor_tensor(
                xo[:], ps[:], xr[:], ALU.add),
                reads=[ps, xr], writes=[xo], tag=f"{name}_fr")
            fw.dma("qsy", xodram[128 * j2:128 * (j2 + 1), qs], xo[:],
                   reads=[xo], writes=[(xodram, f"f{j2}c{q}")], tag=f"{name}_sxf")



def emit_l0ffn(ctx, pool, *, xdram, xodram, lnw_c, lnb_c, wqkvp, woutlp, rtq_c,
               rtkv_c, gamma_c, wqp, wkvt, wouthp, nullkv_r, nullv_row, nullq_c,
               hsel, name):
    fw = ctx.fw
    ctx.build_eye()
    DH, HL = 64, 8
    SC = DH ** -0.5
    MARGIN = 50.0
    nh = [fw.sbuf(f"{name}_nh{k}", [128, T], BF16) for k in range(KD)]
    nl = [fw.sbuf(f"{name}_nl{k}", [128, T], BF16) for k in range(KD)]
    emit_norm(ctx, None, nh, kind="ln", eps=1e-5, name=f"{name}_ln0", pool=pool,
              xdram=xdram)

    lnw = fw.sbuf(f"{name}_lnw", [128, 8], F32)
    lnb = fw.sbuf(f"{name}_lnb", [128, 8], F32)
    gam = fw.sbuf(f"{name}_gam", [128, 8], F32)
    nqc = fw.sbuf(f"{name}_nqc", [128, 8], F32)
    rtq = fw.sbuf(f"{name}_rtq", [128, 8], BF16)
    rtk = fw.sbuf(f"{name}_rtk", [128, 8], BF16)
    nkv = fw.sbuf(f"{name}_nkv", [128, 16], BF16)
    nvr = fw.sbuf(f"{name}_nvr", [1, 520], BF16)
    fw.dma("qgp", nvr[:], nullv_row, reads=[f"{name}nvrd"], writes=[nvr],
           tag=f"{name}_lnvr")
    for (sb, dr, tg) in ((lnw, lnw_c, "lnw"), (lnb, lnb_c, "lnb"), (gam, gamma_c, "gam"),
                         (nqc, nullq_c, "nqc"), (rtq, rtq_c, "rtq"), (rtk, rtkv_c, "rtk"),
                         (nkv, nullkv_r, "nkv")):
        fw.dma("qgp", sb[:], dr, reads=[f"{name}{tg}d"], writes=[sb], tag=f"{name}_l{tg}")

    QT = [fw.sbuf(f"{name}_q{j}", [128, T], BF16) for j in range(KD)]
    KVT = [fw.sbuf(f"{name}_kv{j}", [128, T], BF16) for j in range(4)]
    QH = QT[0:4] if False else None
    vo = [fw.sbuf(f"{name}_vo{i}", [128, HL * (DH + 1)], BF16) for i in range(T // 128)]
    QH = QT[0:4]
    wbuf = [fw.sbuf(f"{name}_wb{i}", [128, 1024], BF16) for i in range(2)]
    PT = [fw.sbuf(f"{name}_pt{i}", [128, 512], BF16) for i in range(2)]
    otc = [fw.sbuf(f"{name}_otc{j}", [128, 512], BF16) for j in range(4)]
    qn2 = fw.sbuf(f"{name}_qn2", [16, T], BF16)
    kmx = fw.sbuf(f"{name}_kmx", [16, 1], F32)
    lrow = fw.sbuf(f"{name}_lrow", [65, 512], F32)
    rrec = lrow
    rbsb = fw.sbuf(f"{name}_rbsb", [64, 512], F32)
    onesrow_b = fw.sbuf(f"{name}_orb", [1, 128], BF16)
    fw.op("gpsimd", lambda e: e.memset(onesrow_b[:], 1.0), writes=[onesrow_b])
    negc_rows = [fw.sbuf(f"{name}_ncr{i}", [1, 512], BF16) for i in range(2)]
    xres = ctx.xstream()[0:2]
    xout = [fw.sbuf(f"{name}_xo{i}", [128, 512], F32) for i in range(2)]
    light_d = fw.dram_scratch(f"{name}_lightd", [D, T], BF16)
    negc_d = fw.dram_scratch(f"{name}_negcd", [16, T], BF16)
    lstr = [fw.sbuf(f"{name}_ls{i}", [128, 512], BF16) for i in range(2)]
    sfx = [fw.sbuf(f"{name}_sf{i}", [128, 512], F32) for i in range(2)]

    nc4 = T // 512
    ntt = T // 128

    emit_norm(ctx, nh, nl, kind="ln", eps=1e-5, name=f"{name}_lnl", pool=pool,
              gamma=[lnw[:, k:k + 1] for k in range(KD)],
              beta=[lnb[:, k:k + 1] for k in range(KD)])
    for ti in range(ntt):
        fw.op("gpsimd", lambda e, ti=ti: e.memset(vo[ti][:, :], 0.0),
              writes=[(vo[ti], "z")], tag=f"{name}_vz")
    for row in range(12):
        wb = wbuf[row % 2]
        fw.dma("qgp", wb[:], wqkvp[128 * row:128 * (row + 1), :], tag=f"{name}_lwqkv")
        if row < 8:
            for c in range(nc4):
                cs = slice(512 * c, 512 * (c + 1))
                ps = pool.take()
                for k in range(KD):
                    fw.op("tensor", lambda e, wb=wb, k=k, cs=cs, ps=ps: e.matmul(
                        ps[:], wb[:, 128 * k:128 * (k + 1)], nl[k][:, cs],
                        start=(k == 0), stop=(k == KD - 1)),
                        reads=[wb, (nl[k], f"c{c}")], writes=[ps], tag=f"{name}_pqk")
                fw.op("scalar", lambda e, row=row, cs=cs, ps=ps, c=c: e.copy(
                    QT[row][:, cs], ps[:]),
                    reads=[ps], writes=[(QT[row], f"c{c}")], tag=f"{name}_eqk")
        else:
            h2 = row - 8
            for ti in range(ntt):
                ts_ = slice(128 * ti, 128 * (ti + 1))
                ps = pool.take()
                for k in range(KD):
                    fw.op("tensor", lambda e, k=k, ts_=ts_, ps=ps, wb=wb: e.matmul(
                        ps[:, 0:128], nl[k][:, ts_], wb[:, 128 * k:128 * (k + 1)],
                        start=(k == 0), stop=(k == KD - 1)),
                        reads=[wb, (nl[k], f"c{ti // 4}")], writes=[ps], tag=f"{name}_pv")
                def ev(e, ti=ti, h2=h2, ps=ps):
                    dst = vo[ti][:, :].rearrange("p (h d) -> p h d", h=HL, d=DH + 1)[
                        :, 2 * h2:2 * h2 + 2, 0:DH]
                    src = ps[:, 0:128].rearrange("p (h d) -> p h d", h=2, d=DH)
                    return e.tensor_copy(dst, src)
                fw.op("vector", ev, reads=[ps, (vo[ti], "z")], writes=[(vo[ti], f"d{h2}")],
                      tag=f"{name}_ev")
    for ti in range(ntt):
        def onesfn(e, ti=ti):
            dst = vo[ti][:, :].rearrange("p (h d) -> p h d", h=HL, d=DH + 1)[:, :, DH:DH + 1]
            return e.memset(dst, 1.0)
        fw.op("gpsimd", onesfn, reads=[(vo[ti], "d0"), (vo[ti], "d1"), (vo[ti], "d2"),
                                       (vo[ti], "d3")],
              writes=[(vo[ti], "ones")], tag=f"{name}_vone")

    sqs = ctx.sqs()
    for (base, tg) in ((4, "kb"), (0, "qb")):
        for c in range(nc4):
            cs = slice(512 * c, 512 * (c + 1))
            ps = pool.take()
            for jj in range(4):
                sq = sqs[jj % 2]
                fw.op("scalar", lambda e, jj=jj, cs=cs, sq=sq, base=base: e.activation(
                    sq[:], QT[base + jj][:, cs], AF.Square),
                    reads=[(QT[base + jj], f"c{c}")], writes=[sq], tag=f"{name}_sq{tg}")
                fw.op("tensor", lambda e, jj=jj, ps=ps, sq=sq: e.matmul(
                    ps[0:16, :], hsel[:, 16 * jj:16 * (jj + 1)], sq[:],
                    start=(jj == 0), stop=(jj == 3)),
                    reads=[hsel, sq], writes=[ps], tag=f"{name}_ns{tg}")
            fw.op("vector", lambda e, ps=ps, cs=cs: e.tensor_copy(qn2[:, cs], ps[0:16, :]),
                  reads=[ps], writes=[(qn2, f"c{c}")], tag=f"{name}_nc{tg}")
        if tg == "kb":
            fw.op("vector", lambda e: e.reduce_max(kmx[:], qn2[:, :], axis=AX.X),
                  reads=[(qn2, f"c{c}") for c in range(nc4)], writes=[kmx],
                  tag=f"{name}_kmax")
            fw.op("scalar", lambda e: e.activation(kmx[:], kmx[:], AF.Sqrt),
                  reads=[kmx], writes=[(kmx, "s")], tag=f"{name}_ksq")
            fw.op("vector", lambda e: e.tensor_scalar(
                kmx[:], kmx[:], -1.05, None, ALU.mult),
                reads=[(kmx, "s")], writes=[(kmx, "n")], tag=f"{name}_kneg")
    fw.op("scalar", lambda e: e.activation(qn2[:, :], qn2[:, :], AF.Sqrt),
          reads=[(qn2, f"c{c}") for c in range(nc4)], writes=[(qn2, "sq")],
          tag=f"{name}_qsq")
    fw.op("vector", lambda e: e.tensor_scalar(
        qn2[:, :], qn2[:, :], kmx[0:16, :], float(MARGIN) / SC, ALU.mult, ALU.add),
        reads=[(qn2, "sq"), (kmx, "n")], writes=[(qn2, "f")], tag=f"{name}_negc")
    fw.dma("qsy", negc_d, qn2[:, :], reads=[(qn2, "f")], writes=[f"{name}_negcd"],
           tag=f"{name}_negcs")

    for ti in range(ntt):
        for h in range(HL):
            jq, hq = h // 2, h % 2
            hp = slice(DH * hq, DH * (hq + 1))
            psO = pool.tiles[2 + (h % 2)]
            pieces = []
            if ti > 0:
                pieces.append(("a", 128 * ti - 64, 64, 64, slice(0, 64)))
            pieces.append(("b", 128 * ti, 128, 0, slice(0, 128)))
            if ti < ntt - 1:
                pieces.append(("c", 128 * ti + 128, 64, 0, slice(64, 128)))
            ncr = negc_rows[(ti * HL + h) % 2]
            fw.dma("qgp", ncr[0:1, 0:128], negc_d[h:h + 1, 128 * ti:128 * (ti + 1)],
                   reads=[f"{name}_negcd"], writes=[ncr], tag=f"{name}_ncl")
            first = True
            lastpn = pieces[-1][0]
            for (pn, k0, kn, pbase, qcols) in pieces:
                psS = pool.tiles[0 if pn != "c" else 1]
                fw.op("tensor", lambda e, jq=jq, hp=hp, k0=k0, kn=kn, ti=ti, psS=psS,
                      qcols=qcols: e.matmul(
                    psS[0:kn, qcols],
                    QT[4 + jq][hp, k0:k0 + kn],
                    QT[jq][hp, 128 * ti + qcols.start:128 * ti + qcols.stop],
                    start=True, stop=False, skip_group_check=True),
                    reads=[(QT[4 + jq], "c0"), (QT[4 + jq], "c1"), (QT[4 + jq], "c2"),
                           (QT[4 + jq], "c3"), (QT[jq], f"c{ti // 4}")],
                    writes=[psS], tag=f"{name}_lst")
                fw.op("tensor", lambda e, ncr=ncr, psS=psS, kn=kn, qcols=qcols: e.matmul(
                    psS[0:kn, qcols], onesrow_b[0:1, 0:kn],
                    ncr[0:1, qcols], start=False, stop=True, skip_group_check=True),
                    reads=[onesrow_b, ncr], writes=[psS], tag=f"{name}_lstc")
                pt = PT[0 if pn != "c" else 1]
                fw.op("scalar", lambda e, psS=psS, pt=pt, kn=kn, qcols=qcols,
                      pbase=pbase: e.activation(
                    pt[pbase:pbase + kn, qcols], psS[0:kn, qcols], AF.Exp, scale=SC),
                    reads=[psS], writes=[pt], tag=f"{name}_lexp")
                voti = ti - 1 if pn == "a" else (ti if pn == "b" else ti + 1)
                fw.op("tensor", lambda e, voti=voti, h=h, pt=pt, psO=psO, kn=kn,
                      pbase=pbase, qcols=qcols, first=first, isl=(pn == lastpn): e.matmul(
                    psO[0:DH + 1, qcols],
                    vo[voti][pbase:pbase + kn, (DH + 1) * h:(DH + 1) * (h + 1)],
                    pt[pbase:pbase + kn, qcols],
                    start=first, stop=isl, skip_group_check=True),
                    reads=[(vo[voti], "d0"), (vo[voti], "d1"), (vo[voti], "d2"),
                           (vo[voti], "d3"), (vo[voti], "ones"), pt],
                    writes=[psO], tag=f"{name}_lav")
                first = False
            fw.op("vector", lambda e, psO=psO: e.tensor_scalar(
                lrow[DH:DH + 1, 0:128], psO[DH:DH + 1, 0:128], 1e-30, None, ALU.max),
                reads=[psO], writes=[lrow], tag=f"{name}_ll")
            fw.op("vector", lambda e: e.reciprocal(rrec[DH:DH + 1, 0:128],
                                                   lrow[DH:DH + 1, 0:128]),
                  reads=[lrow], writes=[lrow], tag=f"{name}_lrec")
            psR = pool.tiles[4]
            fw.op("tensor", lambda e, psR=psR: e.matmul(
                psR[0:DH, 0:128], ctx.ones2_f[DH:DH + 1, 0:DH], rrec[DH:DH + 1, 0:128],
                start=True, stop=True),
                reads=[ctx.ones2_f, lrow], writes=[psR], tag=f"{name}_lrb")
            fw.op("scalar", lambda e, psR=psR: e.copy(rbsb[:, 0:128], psR[0:DH, 0:128]),
                  reads=[psR], writes=[rbsb], tag=f"{name}_lrbe")
            oj, oh = h // 2, h % 2
            fw.op("vector", lambda e, oj=oj, oh=oh, ti=ti, psO=psO: e.tensor_tensor(
                otc[oj][64 * oh:64 * oh + 64, 128 * (ti % 4):128 * (ti % 4) + 128],
                psO[0:DH, 0:128], rbsb[:, 0:128], ALU.mult),
                reads=[psO, rbsb], writes=[(otc[oj], f"h{oh}t{ti % 4}")],
                tag=f"{name}_lon")
        if ti % 4 == 3:
            qc = ti // 4
            qs = slice(512 * qc, 512 * (qc + 1))
            for j2 in range(KD):
                wb = wbuf[j2 % 2]
                fw.dma("qgp", wb[:, 0:512], woutlp[128 * j2:128 * (j2 + 1), :],
                       tag=f"{name}_lwol")
                ps = pool.tiles[5]
                for j in range(4):
                    fw.op("tensor", lambda e, wb=wb, j=j, ps=ps: e.matmul(
                        ps[:], wb[:, 128 * j:128 * (j + 1)], otc[j][:, :],
                        start=(j == 0), stop=(j == 3)),
                        reads=[wb] + [(otc[j], f"h{oh}t{tt}") for oh in range(2)
                                      for tt in range(4)],
                        writes=[ps], tag=f"{name}_plo")
                sf = lstr[j2 % 2]
                fw.op("vector", lambda e, ps=ps, sf=sf: e.tensor_copy(sf[:], ps[:]),
                      reads=[ps], writes=[sf], tag=f"{name}_ple")
                fw.dma("qgp", light_d[128 * j2:128 * (j2 + 1), qs], sf[:],
                       reads=[sf], writes=[(light_d, f"j{j2}q{qc}")], tag=f"{name}_pls")

    srt = [fw.sbuf(f"{name}_srt{i}", [128, 16], F32) for i in range(2)]
    bbt = [fw.sbuf(f"{name}_bbt{i}", [128, 16], F32) for i in range(2)]
    ut = fw.sbuf(f"{name}_ut", [128, 16], F32)
    scp = [fw.sbuf(f"{name}_scp{i}", [128, 16], F32) for i in range(2)]
    selm = [fw.sbuf(f"{name}_selm{i}", [128, 16], F32) for i in range(2)]
    rowm = fw.sbuf(f"{name}_rowm", [128, 1], F32)
    scal = fw.sbuf(f"{name}_scal", [1, 8], F32)
    mcol = fw.sbuf(f"{name}_mcol", [128, 1], F32)
    iot16 = fw.sbuf(f"{name}_io16", [128, 16], F32)
    s_d = fw.dram_scratch(f"{name}_sd", [1, T], F32)
    sel_d = fw.dram_scratch(f"{name}_seld", [2, T], BF16)
    ii16 = fw.sbuf(f"{name}_ii16", [128, 16], I32)
    fw.op("gpsimd", lambda e: e.iota(ii16[:], pattern=[[1, 16]], base=0,
                                     channel_multiplier=16),
          writes=[ii16], tag=f"{name}_io")
    fw.op("vector", lambda e: e.tensor_copy(iot16[:], ii16[:]),
          reads=[ii16], writes=[iot16], tag=f"{name}_ioc")

    for (ridx, rtv, NQ) in ((0, rtq, 512), (1, rtk, 1024)):
        for c in range(nc4):
            cs = slice(512 * c, 512 * (c + 1))
            ps = pool.take()
            for k in range(KD):
                fw.op("tensor", lambda e, k=k, cs=cs, ps=ps, rtv=rtv: e.matmul(
                    ps[0:1, :], rtv[:, k:k + 1], nh[k][:, cs],
                    start=(k == 0), stop=(k == KD - 1)),
                    reads=[rtv, (nh[k], f"c{c}")], writes=[ps], tag=f"{name}_rs")
            fw.op("vector", lambda e, ps=ps, cs=cs: e.tensor_copy(
                lrow[0:1, 0:512], ps[0:1, :]),
                reads=[ps], writes=[lrow], tag=f"{name}_rse")
            fw.dma("qsy", s_d[0:1, cs], lrow[0:1, 0:512],
                   reads=[lrow], writes=[(s_d, f"c{c}")], tag=f"{name}_rss")
        st = srt[ridx]
        fw.dma("qsy", st[:], s_d.rearrange("o (p f) -> (o p) f", p=128, f=16),
               reads=[(s_d, f"c{c}") for c in range(nc4)], writes=[st],
               tag=f"{name}_rsl")
        bb = bbt[ridx]
        fw.op("vector", lambda e, st=st, bb=bb: e.tensor_scalar(
            bb[:], st[:], -1.0, None, ALU.mult),
            reads=[st], writes=[bb], tag=f"{name}_rb0")
        logk = math.log(min(NQ * 9.0 / 8.0, float(T)))
        for it in range(50):
            fw.op("vector", lambda e, st=st, bb=bb: e.tensor_tensor(
                ut[:], st[:], bb[:], ALU.add),
                reads=[st, bb], writes=[ut], tag=f"{name}_r1")
            fw.op("vector", lambda e: e.reduce_max(rowm[:], ut[:], axis=AX.X),
                  reads=[ut], writes=[rowm], tag=f"{name}_r2")
            psm = pool.take()
            fw.op("tensor", lambda e, psm=psm: e.transpose(
                psm[0:1, 0:128], rowm[:], ctx.eye_f[:]),
                reads=[rowm, ctx.eye_f], writes=[psm], tag=f"{name}_r3")
            fw.op("vector", lambda e, psm=psm: e.reduce_max(
                scal[0:1, 0:1], psm[0:1, 0:128], axis=AX.X),
                reads=[psm], writes=[(scal, "m")], tag=f"{name}_r4")
            psb = pool.take()
            fw.op("tensor", lambda e, psb=psb: e.matmul(
                psb[0:128, 0:1], ctx.ones_row_f[:], scal[0:1, 0:1],
                start=True, stop=True),
                reads=[ctx.ones_row_f, (scal, "m")], writes=[psb], tag=f"{name}_r5")
            fw.op("vector", lambda e, psb=psb: e.tensor_scalar(
                mcol[:], psb[0:128, 0:1], -1.0, None, ALU.mult),
                reads=[psb], writes=[mcol], tag=f"{name}_r6")
            fw.op("scalar", lambda e: e.activation(
                ut[:], ut[:], AF.Exp, bias=mcol[:], accum_out=rowm[:]),
                reads=[ut, mcol], writes=[(ut, "e"), rowm], tag=f"{name}_r7")
            psm2 = pool.take()
            fw.op("tensor", lambda e, psm2=psm2: e.matmul(
                psm2[0:1, 0:1], ctx.ones_col_f[:], rowm[:], start=True, stop=True),
                reads=[ctx.ones_col_f, rowm], writes=[psm2], tag=f"{name}_r8")
            fw.op("vector", lambda e, psm2=psm2: e.tensor_copy(
                scal[0:1, 1:2], psm2[0:1, 0:1]),
                reads=[psm2], writes=[(scal, "t")], tag=f"{name}_r9")
            fw.op("scalar", lambda e: e.activation(scal[0:1, 1:2], scal[0:1, 1:2], AF.Ln),
                  reads=[(scal, "t")], writes=[(scal, "lt")], tag=f"{name}_r10")
            fw.op("vector", lambda e: e.tensor_tensor(
                scal[0:1, 7:8], scal[0:1, 0:1], scal[0:1, 1:2], ALU.add),
                reads=[(scal, "m"), (scal, "lt")], writes=[(scal, "ml")],
                tag=f"{name}_r11")
            fw.op("vector", lambda e, logk=logk: e.tensor_scalar(
                scal[0:1, 7:8], scal[0:1, 7:8], -1.0, logk, ALU.mult, ALU.add),
                reads=[(scal, "ml")], writes=[(scal, "a")], tag=f"{name}_r12")
            psb2 = pool.take()
            fw.op("tensor", lambda e, psb2=psb2: e.matmul(
                psb2[0:128, 0:1], ctx.ones_row_f[:], scal[0:1, 7:8],
                start=True, stop=True),
                reads=[ctx.ones_row_f, (scal, "a")], writes=[psb2], tag=f"{name}_r13")
            fw.op("vector", lambda e, psb2=psb2: e.tensor_copy(mcol[:], psb2[0:128, 0:1]),
                  reads=[psb2], writes=[(mcol, "a")], tag=f"{name}_r14")
            fw.op("vector", lambda e, st=st: e.tensor_scalar(
                ut[:], st[:], mcol[:], None, ALU.add),
                reads=[st, (mcol, "a")], writes=[(ut, "sa")], tag=f"{name}_r15")
            fw.op("vector", lambda e, bb=bb: e.tensor_scalar(
                bb[:], ut[:], 0.0, -1.0, ALU.max, ALU.mult),
                reads=[(ut, "sa")], writes=[bb], tag=f"{name}_r16")
        fw.op("vector", lambda e: e.tensor_scalar(
            ut[:], ut[:], 0.0, None, ALU.min),
            reads=[(ut, "sa")], writes=[(ut, "mn")], tag=f"{name}_r17")
        sc = scp[ridx]
        fw.op("scalar", lambda e, sc=sc: e.activation(sc[:], ut[:], AF.Exp),
              reads=[(ut, "mn")], writes=[(sc, "e")], tag=f"{name}_r18")
        EPS2 = 2.0 ** -21
        fw.op("vector", lambda e, sc=sc: e.scalar_tensor_tensor(
            sc[:], iot16[:], -EPS2, sc[:], ALU.mult, ALU.add),
            reads=[iot16, (sc, "e")], writes=[(sc, "p")], tag=f"{name}_r19")
        fw.op("vector", lambda e, sc=sc: e.tensor_scalar(
            sc[:], sc[:], 0.001, None, ALU.add),
            reads=[(sc, "p")], writes=[sc], tag=f"{name}_r20")
        fw.op("gpsimd", lambda e: e.memset(scal[0:1, 2:3], 0.0), writes=[(scal, "lo")],
              tag=f"{name}_b0")
        fw.op("gpsimd", lambda e: e.memset(scal[0:1, 3:4], 1.01), writes=[(scal, "hi")],
              tag=f"{name}_b1")
        for it in range(42):
            fw.op("vector", lambda e: e.tensor_tensor(
                scal[0:1, 4:5], scal[0:1, 2:3], scal[0:1, 3:4], ALU.add),
                reads=[(scal, "lo"), (scal, "hi")], writes=[(scal, "s45")],
                tag=f"{name}_b2")
            fw.op("vector", lambda e: e.tensor_scalar(
                scal[0:1, 4:5], scal[0:1, 4:5], 0.5, None, ALU.mult),
                reads=[(scal, "s45")], writes=[(scal, "mid")], tag=f"{name}_b3")
            psb = pool.take()
            fw.op("tensor", lambda e, psb=psb: e.matmul(
                psb[0:128, 0:1], ctx.ones_row_f[:], scal[0:1, 4:5],
                start=True, stop=True),
                reads=[ctx.ones_row_f, (scal, "mid")], writes=[psb], tag=f"{name}_b4")
            fw.op("vector", lambda e, psb=psb: e.tensor_copy(mcol[:], psb[0:128, 0:1]),
                  reads=[psb], writes=[(mcol, "mid")], tag=f"{name}_b5")
            fw.op("vector", lambda e, sc=sc: e.tensor_scalar(
                ut[:], sc[:], mcol[:], None, ALU.is_gt),
                reads=[sc, (mcol, "mid")], writes=[(ut, "gt")], tag=f"{name}_b6")
            fw.op("vector", lambda e: e.reduce_sum(rowm[:], ut[:], axis=AX.X),
                  reads=[(ut, "gt")], writes=[(rowm, "c")], tag=f"{name}_b7")
            psm2 = pool.take()
            fw.op("tensor", lambda e, psm2=psm2: e.matmul(
                psm2[0:1, 0:1], ctx.ones_col_f[:], rowm[:], start=True, stop=True),
                reads=[ctx.ones_col_f, (rowm, "c")], writes=[psm2], tag=f"{name}_b8")
            fw.op("vector", lambda e, psm2=psm2: e.tensor_copy(
                scal[0:1, 5:6], psm2[0:1, 0:1]),
                reads=[psm2], writes=[(scal, "cnt")], tag=f"{name}_b9")
            fw.op("vector", lambda e, NQ=NQ: e.tensor_scalar(
                scal[0:1, 6:7], scal[0:1, 5:6], float(NQ), None, ALU.is_ge),
                reads=[(scal, "cnt")], writes=[(scal, "pred")], tag=f"{name}_b10")
            fw.op("vector", lambda e: e.tensor_tensor(
                scal[0:1, 0:1], scal[0:1, 4:5], scal[0:1, 2:3], ALU.subtract),
                reads=[(scal, "mid"), (scal, "lo")], writes=[(scal, "s0")],
                tag=f"{name}_b11")
            fw.op("vector", lambda e: e.tensor_scalar(
                scal[0:1, 0:1], scal[0:1, 0:1], scal[0:1, 6:7], None, ALU.mult),
                reads=[(scal, "s0"), (scal, "pred")], writes=[(scal, "s0b")],
                tag=f"{name}_b12")
            fw.op("vector", lambda e: e.tensor_tensor(
                scal[0:1, 2:3], scal[0:1, 2:3], scal[0:1, 0:1], ALU.add),
                reads=[(scal, "lo"), (scal, "s0b")], writes=[(scal, "lo")],
                tag=f"{name}_b13")
            fw.op("vector", lambda e: e.tensor_tensor(
                scal[0:1, 1:2], scal[0:1, 4:5], scal[0:1, 3:4], ALU.subtract),
                reads=[(scal, "mid"), (scal, "hi")], writes=[(scal, "s1")],
                tag=f"{name}_b14")
            fw.op("vector", lambda e: e.tensor_scalar(
                scal[0:1, 7:8], scal[0:1, 6:7], -1.0, 1.0, ALU.mult, ALU.add),
                reads=[(scal, "pred")], writes=[(scal, "np")], tag=f"{name}_b15")
            fw.op("vector", lambda e: e.tensor_scalar(
                scal[0:1, 1:2], scal[0:1, 1:2], scal[0:1, 7:8], None, ALU.mult),
                reads=[(scal, "s1"), (scal, "np")], writes=[(scal, "s1b")],
                tag=f"{name}_b16")
            fw.op("vector", lambda e: e.tensor_tensor(
                scal[0:1, 3:4], scal[0:1, 3:4], scal[0:1, 1:2], ALU.add),
                reads=[(scal, "hi"), (scal, "s1b")], writes=[(scal, "hi")],
                tag=f"{name}_b17")
        psb = pool.take()
        fw.op("tensor", lambda e, psb=psb: e.matmul(
            psb[0:128, 0:1], ctx.ones_row_f[:], scal[0:1, 2:3], start=True, stop=True),
            reads=[ctx.ones_row_f, (scal, "lo")], writes=[psb], tag=f"{name}_b17")
        fw.op("vector", lambda e, psb=psb: e.tensor_copy(mcol[:], psb[0:128, 0:1]),
              reads=[psb], writes=[(mcol, "lo")], tag=f"{name}_b18")
        sm = selm[ridx]
        fw.op("vector", lambda e, sc=sc, sm=sm: e.tensor_scalar(
            sm[:], sc[:], mcol[:], None, ALU.is_gt),
            reads=[sc, (mcol, "lo")], writes=[sm], tag=f"{name}_b19")
        fw.dma("qgp", sel_d[ridx:ridx + 1, :].rearrange("o (p f) -> (o p) f", p=128, f=16),
               sm[:], reads=[sm], writes=[(sel_d, f"r{ridx}")], tag=f"{name}_b20")

    emit_norm(ctx, nh, nl, kind="l2", name=f"{name}_rms", pool=pool,
              gamma=[gam[:, k:k + 1] for k in range(KD)])
    for row in range(4):
        wb = wbuf[row % 2]
        fw.dma("qgp", wb[:], wqp[128 * row:128 * (row + 1), :], tag=f"{name}_lwqh")
        for c in range(nc4):
            cs = slice(512 * c, 512 * (c + 1))
            ps = pool.take()
            for k in range(KD):
                fw.op("tensor", lambda e, wb=wb, k=k, cs=cs, ps=ps: e.matmul(
                    ps[:], wb[:, 128 * k:128 * (k + 1)], nl[k][:, cs],
                    start=(k == 0), stop=(k == KD - 1)),
                    reads=[wb, (nl[k], f"c{c}")], writes=[ps], tag=f"{name}_pqh")
            fw.op("scalar", lambda e, row=row, cs=cs, ps=ps: e.copy(
                QH[row][:, cs], ps[:]),
                reads=[ps], writes=[(QH[row], f"c{c}")], tag=f"{name}_eqh")
    wkvb = [fw.sbuf(f"{name}_wkvb{k}", [128, 1024], BF16) for k in range(KD)]
    for k in range(KD):
        fw.dma("qgp", wkvb[k][:], wkvt[128 * k:128 * (k + 1), :], tag=f"{name}_lwkv")
    for j in range(KD):
        for c in range(nc4):
            cs = slice(512 * c, 512 * (c + 1))
            ps = pool.take()
            for k in range(KD):
                fw.op("tensor", lambda e, j=j, k=k, cs=cs, ps=ps: e.matmul(
                    ps[:], wkvb[k][:, 128 * j:128 * (j + 1)], nl[k][:, cs],
                    start=(k == 0), stop=(k == KD - 1)),
                    reads=[wkvb[k], (nl[k], f"c{c}")], writes=[ps], tag=f"{name}_pkv")
            fw.op("scalar", lambda e, j=j, cs=cs, ps=ps: e.copy(
                KVT[j // 2][64 * (j % 2):64 * (j % 2) + 64, cs], ps[0:64, :]),
                reads=[ps], writes=[(KVT[j // 2], f"h{j % 2}c{c}")], tag=f"{name}_ekv")
    voh = vo
    for ti in range(ntt):
        fw.op("gpsimd", lambda e, ti=ti: e.memset(voh[ti][:, :], 0.0),
              reads=[(voh[ti], "d0"), (voh[ti], "d1"), (voh[ti], "d2"), (voh[ti], "d3"),
                     (voh[ti], "ones")],
              writes=[(voh[ti], "hz")], tag=f"{name}_hvz")
    for ti in range(ntt):
        ts_ = slice(128 * ti, 128 * (ti + 1))
        ps = pool.take()
        for k in range(KD):
            def vmm(e, k=k, ts_=ts_, ps=ps):
                rhs = wkvb[k][:, :].rearrange("p (h two d) -> p h two d",
                                              h=HL, two=2, d=DH)[:, :, 1, :]
                return e.matmul(ps[:], nl[k][:, ts_], rhs,
                                start=(k == 0), stop=(k == KD - 1))
            fw.op("tensor", vmm, reads=[wkvb[k], (nl[k], f"c{ti // 4}")], writes=[ps],
                  tag=f"{name}_hv")
        def hev(e, ti=ti, ps=ps):
            dst = voh[ti][:, :].rearrange("p (h d) -> p h d", h=HL, d=DH + 1)[:, :, 0:DH]
            src = ps[:].rearrange("p (h d) -> p h d", h=HL, d=DH)
            return e.tensor_copy(dst, src)
        fw.op("vector", hev, reads=[ps, (voh[ti], "hz")], writes=[(voh[ti], f"hd")],
              tag=f"{name}_hev")
        def honesfn(e, ti=ti):
            dst = voh[ti][:, :].rearrange("p (h d) -> p h d", h=HL, d=DH + 1)[:, :, DH:DH + 1]
            return e.memset(dst, 1.0)
        fw.op("gpsimd", honesfn, reads=[(voh[ti], "hd")], writes=[(voh[ti], "hones")],
              tag=f"{name}_hvone")
    kvm = fw.sbuf(f"{name}_kvm", [128, 16], F32)
    for kc in range(16):
        fw.dma("qgp", kvm[:, kc:kc + 1],
               sel_d[1:2, 128 * kc:128 * (kc + 1)].rearrange("o (t u) -> (o t) u", u=1),
               reads=[(sel_d, "r1")], writes=[(kvm, f"k{kc}")], tag=f"{name}_kvml")
    for c in range(nc4):
        cs = slice(512 * c, 512 * (c + 1))
        ps = pool.take()
        for jj in range(4):
            sq = sqs[jj % 2]
            fw.op("scalar", lambda e, jj=jj, cs=cs, sq=sq: e.activation(
                sq[:], KVT[jj][:, cs], AF.Square),
                reads=[(KVT[jj], f"h0c{c}"), (KVT[jj], f"h1c{c}")], writes=[sq],
                tag=f"{name}_hksq1")
            fw.op("tensor", lambda e, jj=jj, ps=ps, sq=sq: e.matmul(
                ps[0:16, :], hsel[:, 16 * jj:16 * (jj + 1)], sq[:],
                start=(jj == 0), stop=(jj == 3)),
                reads=[hsel, sq], writes=[ps], tag=f"{name}_hkns")
        fw.op("vector", lambda e, ps=ps, cs=cs: e.tensor_copy(qn2[:, cs], ps[0:16, :]),
              reads=[ps], writes=[(qn2, f"hkc{c}")], tag=f"{name}_hknc")
    fw.op("vector", lambda e: e.reduce_max(kmx[:], qn2[:, :], axis=AX.X),
          reads=[(qn2, f"hkc{c}") for c in range(nc4)], writes=[(kmx, "h")],
          tag=f"{name}_hkmax")
    fw.op("scalar", lambda e: e.activation(sqs[0][0:64, 0:8], nkv[0:64, 0:8], AF.Square),
          reads=[nkv], writes=[sqs[0]], tag=f"{name}_nkq")
    psn0 = pool.take()
    fw.op("tensor", lambda e, psn0=psn0: e.matmul(
        psn0[0:1, 0:8], ctx.ones_col_b[0:64, :], sqs[0][0:64, 0:8],
        start=True, stop=True),
        reads=[ctx.ones_col_b, sqs[0]], writes=[psn0], tag=f"{name}_nk2")
    fw.op("vector", lambda e, psn0=psn0: e.tensor_copy(lrow[0:1, 0:8], psn0[0:1, 0:8]),
          reads=[psn0], writes=[(lrow, "nk")], tag=f"{name}_nk3")
    nk_d = fw.dram_scratch(f"{name}_nkd", [1, 8], F32)
    fw.dma("qsy", nk_d, lrow[0:1, 0:8], reads=[(lrow, "nk")], writes=["nkd" + name],
           tag=f"{name}_nk4")
    nkc = fw.sbuf(f"{name}_nkc", [16, 1], F32)
    fw.op("gpsimd", lambda e: e.memset(nkc[:], 0.0), writes=[(nkc, "z")],
          tag=f"{name}_nk5z")
    fw.dma("qsy", nkc[0:8, :], nk_d.rearrange("o (f u) -> (o f) u", u=1),
           reads=["nkd" + name, (nkc, "z")], writes=[nkc], tag=f"{name}_nk5")
    fw.op("vector", lambda e: e.tensor_scalar(kmx[:], kmx[:], nkc[:], None, ALU.max),
          reads=[(kmx, "h"), nkc, (nkc, "z")], writes=[(kmx, "hm")], tag=f"{name}_nk6")
    fw.op("scalar", lambda e: e.activation(kmx[:], kmx[:], AF.Sqrt),
          reads=[(kmx, "hm")], writes=[(kmx, "hs")], tag=f"{name}_hksq2")
    fw.op("vector", lambda e: e.tensor_scalar(kmx[:], kmx[:], -1.05, None, ALU.mult),
          reads=[(kmx, "hs")], writes=[(kmx, "hn")], tag=f"{name}_hkneg")
    for c in range(nc4):
        cs = slice(512 * c, 512 * (c + 1))
        ps = pool.take()
        for jj in range(4):
            sq = sqs[jj % 2]
            fw.op("scalar", lambda e, jj=jj, cs=cs, sq=sq: e.activation(
                sq[:], QH[jj][:, cs], AF.Square),
                reads=[(QH[jj], f"c{c}")], writes=[sq], tag=f"{name}_sqhq")
            fw.op("tensor", lambda e, jj=jj, ps=ps, sq=sq: e.matmul(
                ps[0:16, :], hsel[:, 16 * jj:16 * (jj + 1)], sq[:],
                start=(jj == 0), stop=(jj == 3)),
                reads=[hsel, sq], writes=[ps], tag=f"{name}_nshq")
        fw.op("vector", lambda e, ps=ps, cs=cs: e.tensor_copy(qn2[:, cs], ps[0:16, :]),
              reads=[ps], writes=[(qn2, f"hhqbc{c}")], tag=f"{name}_nchq")
    fw.op("scalar", lambda e: e.activation(qn2[:, :], qn2[:, :], AF.Sqrt),
          reads=[(qn2, f"hhqbc{c}") for c in range(nc4)], writes=[(qn2, "hsq")],
          tag=f"{name}_hqsq")
    fw.op("vector", lambda e: e.tensor_scalar(
        qn2[:, :], qn2[:, :], kmx[0:16, :], float(MARGIN) / SC, ALU.mult, ALU.add),
        reads=[(qn2, "hsq"), (kmx, "hn")], writes=[(qn2, "hf")], tag=f"{name}_hnegc")
    fw.dma("qsy", negc_d, qn2[:, :], reads=[(qn2, "hf")],
           writes=[f"{name}_negcd", f"{name}_negcd2"], tag=f"{name}_hnegcs")

    for c in range(nc4):
        qs = slice(512 * c, 512 * (c + 1))
        for h in range(HL):
            psO = pool.tiles[2 + (h % 2)]
            ncr = negc_rows[(c * HL + h) % 2]
            fw.dma("qgp", ncr[:], negc_d[h:h + 1, qs],
                   reads=[f"{name}_negcd2"], writes=[ncr], tag=f"{name}_hncl")
            for kc in range(ntt):
                psS = pool.tiles[kc % 2]
                fw.op("tensor", lambda e, h=h, kc=kc, qs=qs, psS=psS: e.matmul(
                    psS[:], KVT[h // 2][64 * (h % 2):64 * (h % 2) + 64,
                                        128 * kc:128 * (kc + 1)],
                    QH[h // 2][64 * (h % 2):64 * (h % 2) + 64, qs],
                    start=True, stop=False, skip_group_check=True),
                    reads=[(KVT[h // 2], f"h{h % 2}c{kc // 4}"), (QH[h // 2], f"c{c}")],
                    writes=[psS], tag=f"{name}_hst")
                fw.op("tensor", lambda e, ncr=ncr, psS=psS: e.matmul(
                    psS[:], onesrow_b[:], ncr[:], start=False, stop=True,
                    skip_group_check=True),
                    reads=[onesrow_b, ncr], writes=[psS], tag=f"{name}_hstc")
                pt = PT[kc % 2]
                fw.op("scalar", lambda e, psS=psS, pt=pt: e.activation(
                    pt[:], psS[:], AF.Exp, scale=SC),
                    reads=[psS], writes=[pt], tag=f"{name}_hexp")
                fw.op("vector", lambda e, pt=pt, kc=kc: e.tensor_scalar(
                    pt[:], pt[:], kvm[:, kc:kc + 1], None, ALU.mult),
                    reads=[pt, (kvm, f"k{kc}")], writes=[(pt, "m")], tag=f"{name}_hmask")
                fw.op("tensor", lambda e, kc=kc, h=h, pt=pt, psO=psO: e.matmul(
                    psO[0:DH + 1, :], voh[kc][:, (DH + 1) * h:(DH + 1) * (h + 1)], pt[:],
                    start=(kc == 0), stop=False, skip_group_check=True),
                    reads=[(voh[kc], "hd"), (voh[kc], "hones"), (pt, "m")],
                    writes=[psO], tag=f"{name}_hav")
            psn = pool.tiles[1]
            fw.op("tensor", lambda e, h=h, qs=qs, psn=psn: e.matmul(
                psn[0:1, :], nkv[64 * (h % 2):64 * (h % 2) + 64, h:h + 1],
                QH[h // 2][64 * (h % 2):64 * (h % 2) + 64, qs],
                start=True, stop=False, skip_group_check=True),
                reads=[nkv, (QH[h // 2], f"c{c}")], writes=[psn], tag=f"{name}_hnst")
            fw.op("tensor", lambda e, ncr=ncr, psn=psn: e.matmul(
                psn[0:1, :], onesrow_b[0:1, 0:1], ncr[:], start=False, stop=True,
                skip_group_check=True),
                reads=[onesrow_b, ncr], writes=[psn], tag=f"{name}_hnstc")
            ptn = PT[1]
            fw.op("scalar", lambda e, psn=psn, ptn=ptn: e.activation(
                ptn[0:1, :], psn[0:1, :], AF.Exp, scale=SC),
                reads=[psn], writes=[(ptn, "n")], tag=f"{name}_hnexp")
            fw.op("tensor", lambda e, h=h, ptn=ptn, psO=psO: e.matmul(
                psO[0:DH + 1, :], nvr[0:1, 65 * h:65 * (h + 1)],
                ptn[0:1, :], start=False, stop=True, skip_group_check=True),
                reads=[nvr, (ptn, "n")], writes=[psO], tag=f"{name}_hnav")
            fw.op("vector", lambda e, psO=psO: e.tensor_scalar(
                lrow[DH:DH + 1, :], psO[DH:DH + 1, :], 1e-30, None, ALU.max),
                reads=[psO], writes=[lrow], tag=f"{name}_hl")
            fw.op("vector", lambda e: e.reciprocal(rrec[DH:DH + 1, :], lrow[DH:DH + 1, :]),
                  reads=[lrow], writes=[lrow], tag=f"{name}_hrec")
            psR = pool.tiles[4]
            fw.op("tensor", lambda e, psR=psR: e.matmul(
                psR[0:DH, :], ctx.ones2_f[DH:DH + 1, 0:DH], rrec[DH:DH + 1, :],
                start=True, stop=True),
                reads=[ctx.ones2_f, lrow], writes=[psR], tag=f"{name}_hrb")
            fw.op("scalar", lambda e, psR=psR: e.copy(rbsb[:], psR[0:DH, :]),
                  reads=[psR], writes=[rbsb], tag=f"{name}_hrbe")
            oj, oh = h // 2, h % 2
            fw.op("vector", lambda e, oj=oj, oh=oh, psO=psO: e.tensor_tensor(
                otc[oj][64 * oh:64 * oh + 64, :], psO[0:DH, :], rbsb[:], ALU.mult),
                reads=[psO, rbsb], writes=[(otc[oj], f"hh{oh}")], tag=f"{name}_hon")
        qmb = sfx[0]
        psq = pool.tiles[5]
        qmr = negc_rows[0]
        fw.dma("qgp", qmr[:], sel_d[0:1, qs], reads=[(sel_d, "r0")], writes=[qmr],
               tag=f"{name}_qml")
        fw.op("tensor", lambda e, psq=psq, qmr=qmr: e.matmul(
            psq[:], onesrow_b[:], qmr[:], start=True, stop=True),
            reads=[onesrow_b, qmr], writes=[psq], tag=f"{name}_qmb")
        fw.op("scalar", lambda e, psq=psq, qmb=qmb: e.copy(qmb[:], psq[:]),
              reads=[psq], writes=[qmb], tag=f"{name}_qme")
        for j2 in range(KD):
            wb = wbuf[j2 % 2]
            fw.dma("qgp", wb[:, 0:512], wouthp[128 * j2:128 * (j2 + 1), :],
                   tag=f"{name}_lwoh")
            ps = pool.tiles[5]
            for j in range(4):
                fw.op("tensor", lambda e, wb=wb, j=j, ps=ps: e.matmul(
                    ps[:], wb[:, 128 * j:128 * (j + 1)], otc[j][:, :],
                    start=(j == 0), stop=(j == 3)),
                    reads=[wb, (otc[j], "hh0"), (otc[j], "hh1")], writes=[ps],
                    tag=f"{name}_pho")
            t1 = sfx[1]
            fw.op("vector", lambda e, ps=ps, j2=j2, t1=t1: e.tensor_scalar(
                t1[:], ps[:], nqc[:, j2:j2 + 1], None, ALU.subtract),
                reads=[ps, nqc], writes=[t1], tag=f"{name}_cb1")
            fw.op("vector", lambda e, t1=t1, qmb=qmb: e.tensor_tensor(
                t1[:], t1[:], qmb[:], ALU.mult),
                reads=[t1, qmb], writes=[(t1, "b")], tag=f"{name}_cb2")
            fw.op("vector", lambda e, t1=t1, j2=j2: e.tensor_scalar(
                t1[:], t1[:], nqc[:, j2:j2 + 1], None, ALU.add),
                reads=[(t1, "b"), nqc], writes=[(t1, "c")], tag=f"{name}_cb3")
            lb = lstr[j2 % 2]
            fw.dma("qsy", lb[:], light_d[128 * j2:128 * (j2 + 1), qs],
                   reads=[(light_d, f"j{j2}q{c}")], writes=[lb], tag=f"{name}_lld")
            fw.op("vector", lambda e, t1=t1, lb=lb: e.tensor_tensor(
                t1[:], t1[:], lb[:], ALU.add),
                reads=[(t1, "c"), lb], writes=[(t1, "d")], tag=f"{name}_cb4")
            xr = xres[j2 % 2]
            fw.dma("qsy", xr[:], xdram[128 * j2:128 * (j2 + 1), qs], tag=f"{name}_lxr")
            xo = xout[j2 % 2]
            fw.op("vector", lambda e, t1=t1, xr=xr, xo=xo: e.scalar_tensor_tensor(
                xo[:], t1[:], 0.5, xr[:], ALU.mult, ALU.add),
                reads=[(t1, "d"), xr], writes=[xo], tag=f"{name}_cb5")
            fw.dma("qsy", xodram[128 * j2:128 * (j2 + 1), qs], xo[:],
                   reads=[xo], writes=[(xodram, f"j{j2}c{c}")], tag=f"{name}_sx")






def _cols(v):
    return np.ascontiguousarray(np.asarray(v, np.float32).reshape(8, 128).T)


def _mk_hsel():
    h = np.zeros((128, 128), dtype=np.float32)
    for j in range(8):
        h[0:64, 16 * j + 2 * j] = 1.0
        h[64:128, 16 * j + 2 * j + 1] = 1.0
    return h.astype(ml_dtypes.bfloat16)


_PROG_CACHE = {}


def _get_prog(key, builder):
    if key not in _PROG_CACHE:
        _PROG_CACHE[key] = builder()
    return _PROG_CACHE[key]


def _mkfw():
    nc = bass.Bass("TRN2", target_bir_lowering=False)
    st = contextlib.ExitStack()
    st.__enter__()
    fw = FW(nc, st)
    ctx = Ctx(fw)
    return nc, st, fw, ctx


def _finish(fw, st):
    fw.emit()
    st.__exit__(None, None, None)


def _b_l0ffn():
    nc, st, fw, ctx = _mkfw()
    xT = fw.dram_in("xT", [D, T], F32)
    dr = {nm: fw.dram_in(nm, shp, BF16 if nm == "hseld" else F32)
          for nm, shp in [("wqkvp", [12 * 128, 1024]), ("woutlp", [8 * 128, 512]),
                          ("wqp", [4 * 128, 1024]), ("wkvt", [1024, 1024]),
                          ("wouthp", [8 * 128, 512]), ("lnwc", [128, 8]),
                          ("lnbc", [128, 8]), ("gamc", [128, 8]), ("nqcc", [128, 8]),
                          ("rtqc", [128, 8]), ("rtkc", [128, 8]), ("nkvr", [128, 16]),
                          ("nvrd", [1, 520]), ("hseld", [128, 128])]}
    oT = fw.dram_out("oT", [D, T], F32)
    hsel = fw.sbuf("hselb", [128, 128], BF16)
    fw.dma("qsy", hsel[:], dr["hseld"], tag="lh")
    pool = PsumPool(fw, [f"pp{i}" for i in range(6)])
    emit_l0ffn(ctx, pool, xdram=xT, xodram=oT, lnw_c=dr["lnwc"], lnb_c=dr["lnbc"],
               wqkvp=dr["wqkvp"], woutlp=dr["woutlp"], rtq_c=dr["rtqc"],
               rtkv_c=dr["rtkc"], gamma_c=dr["gamc"], wqp=dr["wqp"],
               wkvt=dr["wkvt"], wouthp=dr["wouthp"], nullkv_r=dr["nkvr"],
               nullv_row=dr["nvrd"], nullq_c=dr["nqcc"], hsel=hsel, name="f0")
    _finish(fw, st)
    return nc


def _b_l0attn():
    nc, st, fw, ctx = _mkfw()
    xT = fw.dram_in("xT", [D, T], F32)
    wqp = fw.dram_in("wqp", [8 * 128, 1024], F32)
    wkp = fw.dram_in("wkp", [8 * 128, 1024], F32)
    wvt = fw.dram_in("wvt", [1024, 1024], F32)
    wop = fw.dram_in("wop", [8 * 128, 1024], F32)
    bo_d = fw.dram_in("bo", [128, 8], F32)
    hsel_d = fw.dram_in("hseld", [128, 128], BF16)
    oT = fw.dram_out("oT", [D, T], F32)
    hsel = fw.sbuf("hselb", [128, 128], BF16)
    bo = fw.sbuf("bosb", [128, 8], F32)
    fw.dma("qsy", hsel[:], hsel_d, tag="lh")
    fw.dma("qsy", bo[:], bo_d, tag="lb")
    pool = PsumPool(fw, [f"pp{i}" for i in range(6)])
    emit_l0attn(ctx, pool, xdram=xT, xodram=oT, wqp=wqp, wkp=wkp, wvt=wvt,
                wop=wop, bo=bo, hsel=hsel, margin=50.0, name="a0")
    _finish(fw, st)
    return nc


def _b_l1route():
    nc, st, fw, ctx = _mkfw()
    xT = fw.dram_in("xT", [D, T], F32)
    rwt = fw.dram_in("rwt", [1024, 8], F32)
    rb = fw.dram_in("rb", [1, 8], F32)
    wts_out = fw.dram_out("wts_out", [T, 8], F32)
    pc_out = fw.dram_out("pc_out", [1, 16], F32)
    pool = PsumPool(fw, [f"pp{i}" for i in range(6)])
    emit_l1ffn_route(ctx, pool, xdram=xT, rwt=rwt, rbrow_d=rb,
                     wts_out=wts_out, pc_out=pc_out, name="r1")
    _finish(fw, st)
    return nc


def _b_l1ffn():
    nc, st, fw, ctx = _mkfw()
    xT = fw.dram_in("xT", [D, T], F32)
    wts_in = fw.dram_in("wts_in", [T, 8], F32)
    allC = fw.dram_in("allC", [8, 16], F32)
    cf = fw.dram_in("cf", [1, 1], F32)
    w1p = [fw.dram_in(f"w1p{e}", [8 * 128, 1024], F32) for e in range(8)]
    w2p = [fw.dram_in(f"w2p{e}", [8 * 128, 1024], F32) for e in range(8)]
    b1c = fw.dram_in("b1c", [128, 64], F32)
    b2c = fw.dram_in("b2c", [128, 64], F32)
    oT = fw.dram_out("oT", [D, T], F32)
    pool = PsumPool(fw, [f"pp{i}" for i in range(6)])
    emit_l1ffn_ffn(ctx, pool, xdram=xT, xodram=oT, wts_in=wts_in, allC=allC,
                   coreflag=cf, w1p=w1p, w2p=w2p, b1c=b1c, b2c=b2c, name="f1")
    _finish(fw, st)
    return nc


def _b_l1attn():
    nc, st, fw, ctx = _mkfw()
    xT = fw.dram_in("xT", [D, T], F32)
    wqp = fw.dram_in("wqp", [8 * 128, 1024], F32)
    wkp = fw.dram_in("wkp", [8 * 128, 1024], F32)
    wvt = fw.dram_in("wvt", [1024, 1024], F32)
    hsel_d = fw.dram_in("hseld", [128, 128], BF16)
    oT = fw.dram_out("oT", [D, T], F32)
    hsel = fw.sbuf("hselb", [128, 128], BF16)
    fw.dma("qsy", hsel[:], hsel_d, tag="lh")
    pool = PsumPool(fw, [f"pp{i}" for i in range(6)])
    emit_attention(ctx, pool, xdram=xT, xodram=oT, wqp=wqp, wkp=wkp, wvt=wvt,
                   hsel=hsel, scale=0.125, causal=False, margin=50.0, name="a1")
    _finish(fw, st)
    return nc


def _b_l2ffn():
    nc, st, fw, ctx = _mkfw()
    xT = fw.dram_in("xT", [D, T], F32)
    w1p = fw.dram_in("w1p", [32 * 128, 1024], F32)
    w2p = fw.dram_in("w2p", [32 * 128, 1024], F32)
    h1p = fw.dram_in("h1p", [32 * 128, 4096], F32)
    h2p = fw.dram_in("h2p", [32 * 128, 4096], F32)
    w3p = fw.dram_in("w3p", [8 * 128, 4096], F32)
    scl = fw.dram_in("scl", [128, 32 * 4 + 8], F32)
    oT = fw.dram_out("oT", [D, T], F32)
    w1s = fw.dram_scratch("w1s", [32 * 128, 1024], BF16)
    w2s = fw.dram_scratch("w2s", [32 * 128, 1024], BF16)
    h1s = fw.dram_scratch("h1s", [32 * 128, 4096], BF16)
    h2s = fw.dram_scratch("h2s", [32 * 128, 4096], BF16)
    w3s = fw.dram_scratch("w3s", [8 * 128, 4096], BF16)
    xt = [fw.sbuf(f"xt{k}", [128, T], F32) for k in range(KD)]
    sclsb = fw.sbuf("sclsb", [128, 32 * 4 + 8], F32)
    s3h = fw.sbuf("s3h", [128, 8], F32)
    for k in range(KD):
        fw.dma("qsy", xt[k][:], xT[128 * k:128 * (k + 1), :], tag="ldx")
    fw.dma("qsy", sclsb[:], scl, tag="lds")
    fw.op("vector", lambda e: e.tensor_scalar(s3h[:], sclsb[:, 128:136], 0.5, None,
                                              ALU.mult),
          reads=[sclsb], writes=[s3h])
    castb = [fw.sbuf(f"castb{i}", [128, 4096], BF16) for i in range(2)]
    emit_cast_dram(fw, w1s, w1p, 32 * 128, 1024, "cw1", bufs=castb)
    emit_cast_dram(fw, w2s, w2p, 32 * 128, 1024, "cw2", bufs=castb)
    emit_cast_dram(fw, h1s, h1p, 32 * 128, 4096, "ch1", bufs=castb)
    emit_cast_dram(fw, h2s, h2p, 32 * 128, 4096, "ch2", bufs=castb)
    emit_cast_dram(fw, w3s, w3p, 8 * 128, 4096, "cw3", bufs=castb)
    pool = PsumPool(fw, [f"pp{i}" for i in range(6)])
    scales = {"s1": sclsb[:, 0:32], "s2": sclsb[:, 32:64], "sh1": sclsb[:, 64:96],
              "sh2": sclsb[:, 96:128], "s3h": s3h[:]}
    emit_l2ffn(ctx, pool, xt=xt, w1s=w1s, w2s=w2s, h1s=h1s, h2s=h2s, w3s=w3s,
               scales_sb=scales, name="f2")
    for k in range(KD):
        fw.dma("qgp", oT[128 * k:128 * (k + 1), :], xt[k][:],
               reads=[xt[k]], writes=["oT"], tag="st")
    _finish(fw, st)
    return nc


def _b_l2attn():
    nc, st, fw, ctx = _mkfw()
    xT = fw.dram_in("xT", [D, T], F32)
    wqp = fw.dram_in("wqp", [1024, 1024], F32)
    wkp = fw.dram_in("wkp", [1024, 1024], F32)
    wvp = fw.dram_in("wvp", [1024, 1024], F32)
    oT = fw.dram_out("oT", [D, T], F32)
    pool = PsumPool(fw, [f"pp{i}" for i in range(6)])
    emit_l2attn(ctx, pool, xdram=xT, xodram=oT, wqp=wqp, wkp=wkp, wvp=wvp, name="a2")
    _finish(fw, st)
    return nc


def _b_l3ffn():
    nc, st, fw, ctx = _mkfw()
    xT = fw.dram_in("xT", [D, T], F32)
    gwt = fw.dram_in("gwt", [1024, 8], F32)
    w1p = [fw.dram_in(f"w1p{e}", [32 * 128, 1024], F32) for e in range(8)]
    w3p = [fw.dram_in(f"w3p{e}", [32 * 128, 1024], F32) for e in range(8)]
    w2p = [fw.dram_in(f"w2p{e}", [8 * 128, 4096], F32) for e in range(8)]
    oT = fw.dram_out("oT", [D, T], F32)
    pool = PsumPool(fw, [f"pp{i}" for i in range(6)])
    emit_l3ffn(ctx, pool, xdram=xT, xodram=oT, gwt=gwt, w1p=w1p, w3p=w3p, w2p=w2p,
               name="f3")
    _finish(fw, st)
    return nc


def _b_l3attn():
    nc, st, fw, ctx = _mkfw()
    xT = fw.dram_in("xT", [D, T], F32)
    dr = {nm: fw.dram_in(nm, shp, F32)
          for nm, shp in [("qkvp", [24 * 128, 1024]), ("opp", [8 * 128, 1024]),
                          ("ogp", [8 * 128, 1024]), ("pwp", [8 * 128, 1024]),
                          ("gatep", [8 * 128, 2048]), ("w1p", [22 * 128, 1024]),
                          ("w2p", [22 * 128, 1024]), ("w3p", [8 * 128, 2816]),
                          ("ogbc", [128, 8]), ("cbc", [128, 8]), ("n1c", [128, 8]),
                          ("n2c", [128, 8]), ("dpd", [1, 8]), ("cwv", [128, 32])]}
    oT = fw.dram_out("oT", [D, T], F32)
    pool = PsumPool(fw, [f"pp{i}" for i in range(6)])
    emit_l3attn(ctx, pool, xdram=xT, xodram=oT, qkvp=dr["qkvp"], opp=dr["opp"],
                ogp=dr["ogp"], ogb_c=dr["ogbc"], dp_d=dr["dpd"], cwv_d=dr["cwv"],
                cb_c=dr["cbc"], pwp=dr["pwp"], gatep=dr["gatep"], w1p=dr["w1p"],
                w2p=dr["w2p"], w3p=dr["w3p"], n1c=dr["n1c"], n2c=dr["n2c"], name="a3")
    _finish(fw, st)
    return nc


def _run(nc, per_core, shared):
    in_maps = []
    for b in range(8):
        m = dict(shared)
        m.update(per_core[b])
        in_maps.append(m)
    return run_bass_kernel_spmd(nc, in_maps, core_ids=list(range(8)))


def kernel(**inputs):
    I = {k: np.asarray(v) for k, v in inputs.items()}
    x = np.asarray(I["L0_attn_input_seq_embs"], np.float32)
    xTs = [np.ascontiguousarray(x[b].T) for b in range(8)]
    hseld = _mk_hsel()
    pk = lambda a: pack2(np.ascontiguousarray(np.asarray(a, np.float32)))
    pkT = lambda a: pack2(np.ascontiguousarray(np.asarray(a, np.float32).T))

    H = 8
    nkvr = np.zeros((128, 16), np.float32)
    nvrd = np.zeros((1, 520), np.float32)
    nkv = np.asarray(I["L0_ffn_null_kv"], np.float32)
    for h in range(H):
        nkvr[0:64, h] = nkv[0, h]
        nkvr[64:128, h] = nkv[0, h]
        nvrd[0, 65 * h:65 * h + 64] = nkv[1, h]
        nvrd[0, 65 * h + 64] = 1.0
    w_l0ffn = {"wqkvp": pkT(I["L0_ffn_Wqkv"]), "woutlp": pkT(I["L0_ffn_Wout_l"]),
               "wqp": pkT(I["L0_ffn_Wq"]), "wkvt": np.ascontiguousarray(
                   np.asarray(I["L0_ffn_Wkv"], np.float32).T),
               "wouthp": pkT(I["L0_ffn_Wout_h"]), "lnwc": _cols(I["L0_ffn_ln_w"]),
               "lnbc": _cols(I["L0_ffn_ln_b"]), "gamc": _cols(I["L0_ffn_gamma"]),
               "nqcc": _cols(I["L0_ffn_null_q"]),
               "rtqc": _cols(np.asarray(I["L0_ffn_rt_q"], np.float32)[0]),
               "rtkc": _cols(np.asarray(I["L0_ffn_rt_kv"], np.float32)[0]),
               "nkvr": nkvr, "nvrd": nvrd, "hseld": hseld}
    w_l0attn = {"wqp": pkT(I["L0_attn_W_Q"]), "wkp": pkT(I["L0_attn_W_K"]),
                "wvt": np.ascontiguousarray(np.asarray(I["L0_attn_W_V"], np.float32).T),
                "wop": pkT(I["L0_attn_W_O"]), "bo": _cols(I["L0_attn_b_O"]),
                "hseld": hseld}
    k3 = np.asarray(I["L1_attn_kernel"], np.float32)
    w_l1attn = {"wqp": pk(k3[0]), "wkp": pk(k3[1]),
                "wvt": np.ascontiguousarray(k3[2]), "hseld": hseld}
    b1 = np.asarray(I["L1_ffn_b1"], np.float32)
    b2 = np.asarray(I["L1_ffn_b2"], np.float32)
    b1c = np.zeros((128, 64), np.float32)
    b2c = np.zeros((128, 64), np.float32)
    for e in range(8):
        for j in range(8):
            b1c[:, 8 * e + j] = b1[e, 128 * j:128 * (j + 1)]
            b2c[:, 8 * e + j] = b2[e, 128 * j:128 * (j + 1)]
    w_l1route = {"rwt": np.ascontiguousarray(np.asarray(I["L1_ffn_router_w"], np.float32)),
                 "rb": np.asarray(I["L1_ffn_router_b"], np.float32).reshape(1, 8)}
    w_l1ffn = {"b1c": b1c, "b2c": b2c}
    for e in range(8):
        w_l1ffn[f"w1p{e}"] = pk(np.asarray(I["L1_ffn_w1"], np.float32)[e])
        w_l1ffn[f"w2p{e}"] = pk(np.asarray(I["L1_ffn_w2"], np.float32)[e])
    w_l2attn = {"wqp": np.ascontiguousarray(np.asarray(I["L2_attn_W_Q"], np.float32).T),
                "wkp": np.ascontiguousarray(np.asarray(I["L2_attn_W_K"], np.float32).T),
                "wvp": np.ascontiguousarray(np.asarray(I["L2_attn_W_V"], np.float32).T)}
    sc_parts = [np.asarray(I[k], np.float32).reshape(-1, 128).T
                for k in ("L2_ffn_w1_s", "L2_ffn_w2_s", "L2_ffn_h1_s", "L2_ffn_h2_s",
                          "L2_ffn_w3_s")]
    w_l2ffn = {"w1p": pkT(I["L2_ffn_w1_w"]), "w2p": pkT(I["L2_ffn_w2_w"]),
               "h1p": pkT(I["L2_ffn_h1_w"]), "h2p": pkT(I["L2_ffn_h2_w"]),
               "w3p": pkT(I["L2_ffn_w3_w"]),
               "scl": np.ascontiguousarray(np.concatenate(sc_parts, axis=1))}
    cw = np.asarray(I["L3_attn_conv_w"], np.float32)
    cwv = np.zeros((128, 32), np.float32)
    for k in range(8):
        for i in range(4):
            cwv[:, 4 * k + i] = cw[128 * k:128 * (k + 1), 0, i]
    w_l3attn = {"qkvp": pkT(I["L3_attn_qkv_w"]), "opp": pkT(I["L3_attn_out_proj_w"]),
                "ogp": pkT(I["L3_attn_out_gate_w"]), "pwp": pkT(I["L3_attn_pw_w"]),
                "gatep": pkT(I["L3_attn_gate_w"]), "w1p": pkT(I["L3_attn_w1"]),
                "w2p": pkT(I["L3_attn_w2"]), "w3p": pkT(I["L3_attn_w3"]),
                "ogbc": _cols(I["L3_attn_out_gate_b"]), "cbc": _cols(I["L3_attn_conv_b"]),
                "n1c": _cols(I["L3_attn_norm1_w"]), "n2c": _cols(I["L3_attn_norm2_w"]),
                "dpd": np.asarray(I["L3_attn_decay_param"], np.float32).reshape(1, 8),
                "cwv": cwv}
    w_l3ffn = {"gwt": np.ascontiguousarray(np.asarray(I["L3_ffn_gate_w"], np.float32))}
    for e in range(8):
        w_l3ffn[f"w1p{e}"] = pk(np.asarray(I["L3_ffn_w1"], np.float32)[e])
        w_l3ffn[f"w3p{e}"] = pk(np.asarray(I["L3_ffn_w3"], np.float32)[e])
        w_l3ffn[f"w2p{e}"] = pk(np.asarray(I["L3_ffn_w2"], np.float32)[e])

    def launch(key, builder, shared, xts):
        nc = _get_prog(key, builder)
        res = _run(nc, [{"xT": xts[b]} for b in range(8)], shared)
        return [np.asarray(res.results[b]["oT"]) for b in range(8)]

    def l1ffn_step(xts):
        ncr = _get_prog("l1route", _b_l1route)
        res = _run(ncr, [{"xT": xts[b]} for b in range(8)], w_l1route)
        wts = [np.asarray(res.results[b]["wts_out"]) for b in range(8)]
        pcs = [np.asarray(res.results[b]["pc_out"]).reshape(16) for b in range(8)]
        allC = np.ascontiguousarray(np.stack(pcs, axis=0))
        ncf = _get_prog("l1ffn", _b_l1ffn)
        per_core = [{"xT": xts[b], "wts_in": wts[b], "allC": allC,
                     "cf": np.full((1, 1), 1.0 if b == 0 else 0.0, np.float32)}
                    for b in range(8)]
        res = _run(ncf, per_core, w_l1ffn)
        return [np.asarray(res.results[b]["oT"]) for b in range(8)]

    xts = xTs
    xts = launch("l0ffn", _b_l0ffn, w_l0ffn, xts)
    xts = launch("l0attn", _b_l0attn, w_l0attn, xts)
    xts = launch("l0ffn", _b_l0ffn, w_l0ffn, xts)
    xts = l1ffn_step(xts)
    xts = launch("l1attn", _b_l1attn, w_l1attn, xts)
    xts = l1ffn_step(xts)
    xts = launch("l2ffn", _b_l2ffn, w_l2ffn, xts)
    xts = launch("l2attn", _b_l2attn, w_l2attn, xts)
    xts = launch("l2ffn", _b_l2ffn, w_l2ffn, xts)
    xts = launch("l3ffn", _b_l3ffn, w_l3ffn, xts)
    xts = launch("l3attn", _b_l3attn, w_l3attn, xts)
    xts = launch("l3ffn", _b_l3ffn, w_l3ffn, xts)

    out = np.stack([np.ascontiguousarray(np.asarray(xts[b], np.float32).T)
                    for b in range(8)], axis=0)
    return out.astype(np.float32)
